# version 07 (final)
# baseline (speedup 1.0000x reference)
import contextlib
import math

import numpy as np

import concourse.bass as bass
import concourse.mybir as mybir
from concourse.bass_utils import run_bass_kernel_spmd

F32 = mybir.dt.float32
BF16 = mybir.dt.bfloat16
AX = mybir.AxisListType.X
OP = mybir.AluOpType
AF = mybir.ActivationFunctionType

S = 2048
D = 1024
NT = S // 128
ND = D // 128
EPS_LN = 1e-5
EPS_RMS0 = 1.1920929e-07
EPS_RMS1 = 1e-6
NEG = -1.0e30

H_MLP = 4096
H_FF0 = 2048
E0R, E0S = 14, 2
NH, NKV, HD = 16, 8, 64
M1 = 3584
E1 = 8
ROPE_THETA = 10000.0



class Ctr:

    def __init__(self, es, nc, name):
        self.h = es.enter_context(nc.semaphore(name))
        self.n = 0


class Plan:

    ENGINES = ("sync", "gpsimd", "scalar", "vector", "tensor")

    def __init__(self, g):
        self.g = g
        self.ops = []

    def op(self, engine, fn, waits=(), inc=None):
        own = {"vector": self.g.c.dve, "scalar": self.g.c.act}.get(engine)
        if own is not None:
            if inc is None:
                inc = (own, 1)
            assert inc[0] is own, f"{engine} op must inc its own ctr"
            waits = tuple(waits) + ((own, own.n),)
        if inc is not None:
            ctr, k = inc
            ctr.n += k
        self.ops.append((engine, fn, tuple(waits), inc))
        if inc is not None:
            return inc[0].n
        return None

    def emit(self, nc):
        with nc.Block() as blk:
            for en in self.ENGINES:
                ops = [o for o in self.ops if o[0] == en]
                if not ops:
                    continue

                def body(eng, ops=ops):
                    waited = {}
                    for _, fn, waits, inc in ops:
                        for (ctr, val) in waits:
                            if waited.get(id(ctr), -1) >= val:
                                continue
                            eng.wait_ge(ctr.h, val)
                            waited[id(ctr)] = val
                        ins = fn(eng)
                        if inc is not None:
                            ins.then_inc(inc[0].h, inc[1])

                getattr(blk, en)(body)



def ln_stats(p, g, src_ap_1024, sqdump, c8, eps_col,
             waits_v=(), waits_s=()):
    v, a = g.c.dve, g.c.act
    rstd, nb = c8[:, 0:1], c8[:, 1:2]
    ssq, sm, mu, var, t6 = (c8[:, i:i + 1] for i in range(2, 7))
    p.op('scalar', lambda e: e.activation(sqdump, src_ap_1024, AF.Square,
                                          accum_out=ssq),
         waits=waits_s, inc=(a, 1))
    mka = a.n
    p.op('vector', lambda e: e.reduce_sum(sm, src_ap_1024, axis=AX),
         waits=waits_v)
    p.op('vector', lambda e: e.tensor_scalar(mu, sm, 1.0 / D, None, OP.mult))
    p.op('vector', lambda e: e.tensor_scalar(var, ssq, 1.0 / D, None, OP.mult),
         waits=((a, mka),))
    p.op('vector', lambda e: e.tensor_mul(t6, mu, mu))
    p.op('vector', lambda e: e.tensor_sub(var, var, t6), inc=(v, 1))
    p.op('scalar', lambda e: e.activation(rstd, var, AF.Sqrt, bias=eps_col),
         waits=((v, v.n),), inc=(a, 1))
    p.op('vector', lambda e: e.reciprocal(rstd, rstd), waits=((a, a.n),))
    p.op('vector', lambda e: e.tensor_scalar(nb, mu, rstd, -1.0,
                                             OP.mult, OP.mult), inc=(v, 1))
    return v.n, a.n



class G:
    pass


def build_program(upto="l1f", dbg=None):
    nc = bass.Bass("TRN2", target_bir_lowering=False,
                   detect_race_conditions=False)
    g = G()
    g.nc = nc
    g.es = contextlib.ExitStack()

    class C:
        pass

    g.c = C()
    for name in ("ld", "cast", "act", "dve", "pe", "st", "ld2", "pool"):
        setattr(g.c, name, Ctr(g.es, nc, "c_" + name))

    dt = {}

    def din(name, shape):
        dt[name] = nc.dram_tensor(name, list(shape), F32, kind="ExternalInput")
        return dt[name]

    x_in = din("x_in", [S, D])
    y_out = nc.dram_tensor("y_out", [S, D], F32, kind="ExternalOutput")

    din("Wa", [D, D]); din("Wb", [D, D]); din("Wc", [D, D])
    din("W1", [D, H_MLP]); din("W2", [H_MLP, D])
    for v in ("g1", "b1", "ba", "bb", "bc", "g2", "b2", "bm2"):
        din(v + "_pc", [128, 8])
    din("bm2", [D])
    din("bm1_pc", [128, 32])
    din("rw0_pc", [128, ND, E0R])
    din("rms0_pc", [128, E0R * 8]); din("rms0s_pc", [128, E0S * 8])
    din("f0w1", [E0R, D, H_FF0]); din("f0w2", [E0R, H_FF0, D])
    din("f0w1s", [E0S, D, H_FF0]); din("f0w2s", [E0S, H_FF0, D])
    din("Wq", [D, D]); din("Wk", [D, D]); din("Wv", [D, D])
    din("ln1_pc", [128, 8]); din("ln2_pc", [128, 8])
    din("wq1", [D, NH * HD]); din("wk1", [D, NKV * HD]); din("wv1", [D, NKV * HD])
    din("wo1", [NH * HD, D])
    din("rw1_pc", [128, ND, E1])
    din("wg1", [E1, D, M1]); din("wu1", [E1, D, M1]); din("wd1", [E1, M1, D])
    din("ident", [128, 128])
    din("causal", [128, 128])
    din("ropeq_cos", [128, S]); din("ropeq_sin", [128, S])
    din("ropek_cos", [128, S]); din("ropek_sin", [128, S])
    din("onesr", [1, 128])

    g.dt = dt
    g.y_out = y_out

    scr = {}
    for n in ("sx1", "sx2", "sx3", "sxn4"):
        scr[n] = nc.dram_tensor(n, [S, D], F32, kind="Internal")
    g.scr = scr

    sb = nc.alloc_sbuf_tensor
    g.ident32 = sb("ident32", [128, 128], F32)
    g.colv = sb("colv", [128, 8 * 8], F32)
    g.bm1c = sb("bm1c", [128, 32], F32)
    g.rmsc = sb("rmsc", [128, (E0R + E0S) * 8], F32)
    g.lnc = sb("lnc", [128, 16], F32)
    g.onesr = sb("onesr_s", [1, 128], F32)
    g.bm2row = sb("bm2row", [1, D], F32)
    g.epsc = sb("epsc", [128, 4], F32)
    g.ident16 = sb("ident16", [128, 128], BF16)
    g.causal32 = sb("causal32", [128, 128], F32)

    def colslice(i):
        return g.colv[:, i * 8:(i + 1) * 8]

    g.g1c, g.b1c, g.bac, g.bbc, g.bcc, g.g2c, g.b2c, g.bm2c = (
        colslice(i) for i in range(8))

    p = Plan(g)
    ld = g.c.ld

    def dma(dst, src):
        return lambda e: e.dma_start(dst, src)

    p.op('sync', dma(g.ident32[:, :], dt["ident"][:, :]), inc=(ld, 16))
    vecs = ["g1", "b1", "ba", "bb", "bc", "g2", "b2", "bm2"]
    for i, vname in enumerate(vecs):
        p.op('sync', dma(colslice(i), dt[vname + "_pc"][:, :]), inc=(ld, 16))
    p.op('sync', dma(g.bm1c[:, :], dt["bm1_pc"][:, :]), inc=(ld, 16))
    p.op('sync', dma(g.rmsc[:, 0:E0R * 8], dt["rms0_pc"][:, :]), inc=(ld, 16))
    p.op('sync', dma(g.rmsc[:, E0R * 8:(E0R + E0S) * 8], dt["rms0s_pc"][:, :]),
         inc=(ld, 16))
    p.op('sync', dma(g.lnc[:, 0:8], dt["ln1_pc"][:, :]), inc=(ld, 16))
    p.op('sync', dma(g.lnc[:, 8:16], dt["ln2_pc"][:, :]), inc=(ld, 16))
    p.op('sync', dma(g.onesr[:, :], dt["onesr"][:, :]), inc=(ld, 16))
    p.op('sync', dma(g.causal32[:, :], dt["causal"][:, :]), inc=(ld, 16))
    p.op('gpsimd', dma(g.ident16[:, :], dt["ident"][:, :]), inc=(g.c.cast, 16))
    for i, v in enumerate((EPS_LN, EPS_RMS0, EPS_RMS1, 1e-9)):
        p.op('gpsimd', (lambda e, i=i, v=v: e.memset(g.epsc[:, i:i + 1], v)),
             inc=(g.c.pool, 1))
    p.op('sync', dma(g.bm2row[:, :], dt["bm2"].rearrange("(o d) -> o d", o=1)), inc=(ld, 16))
    need = ld.n
    poolneed = g.c.pool.n
    castneed = g.c.cast.n
    p.op('sync', lambda e: e.nop(),
         waits=((ld, need), (g.c.pool, poolneed), (g.c.cast, castneed)))
    p.emit(nc)

    order = ["l0a", "l0f", "l1a", "l1f"]
    upto_i = order.index(upto)
    chain = [x_in, scr["sx1"], scr["sx2"], scr["sx3"], None]

    for i, ph in enumerate(order[:upto_i + 1]):
        src = chain[i]
        dst = y_out if i == upto_i else chain[i + 1]
        if ph == "l0a":
            phase_l0a(g, src, dst, dbg=dbg)
        elif ph == "l0f":
            phase_l0f(g, src, dst, dbg=dbg)
        elif ph == "l1a":
            phase_l1a(g, src, dst, dbg=dbg)
        elif ph == "l1f":
            phase_l1f(g, src, dst, dbg=dbg)

    return nc, g



def phase_l0a(g, x_src, x_dst, dbg=None):
    nc = g.nc
    dt = g.dt
    ld, cast, act, dve, pe, st = (g.c.ld, g.c.cast, g.c.act, g.c.dve,
                                  g.c.pe, g.c.st)
    xs = x_src.rearrange("(t p) d -> p t d", p=128)
    xd = x_dst.rearrange("(t p) d -> p t d", p=128)

    with contextlib.ExitStack() as ph:
        sbuf = lambda n, s, d=F32: ph.enter_context(nc.sbuf_tensor(n, s, d))
        psum = lambda n, s, d=F32: ph.enter_context(nc.psum_tensor(n, s, d))

        u = sbuf("l0a_u", [128, NT, D])
        hT = sbuf("l0a_hT", [128, ND, S], BF16)
        sqd = sbuf("l0a_sqd", [128, D])
        cols = sbuf("l0a_cols", [128, 32])
        bm2b = sbuf("l0a_bm2b", [128, D])

        def dump_u():
            pd = Plan(g)
            for t in range(NT):
                pd.op('sync', (lambda e, t=t: e.dma_start(xd[:, t, :],
                                                          u[:, t, :])),
                      inc=(st, 16))
            need = st.n
            pd.op('sync', lambda e: e.nop(), waits=((st, need),))
            pd.emit(nc)

        s12 = ph.enter_context(contextlib.ExitStack())
        wabc = s12.enter_context(
            nc.sbuf_tensor("l0a_wabc", [128, ND, 3 * D], BF16))

        p = Plan(g)
        for mi, wn in enumerate(("Wa", "Wb", "Wc")):
            p.op('gpsimd', (lambda e, mi=mi, wn=wn: e.dma_start(
                wabc[:, :, mi * D:(mi + 1) * D],
                dt[wn].rearrange("(c p) e -> p c e", p=128))),
                inc=(cast, 16))
        wabc_mark = cast.n

        with contextlib.ExitStack() as b1:
            xstg = b1.enter_context(nc.sbuf_tensor("l0a_xstg", [128, 2, D], F32))
            nrm = b1.enter_context(nc.sbuf_tensor("l0a_nrm", [128, 2, D], F32))
            pt = [b1.enter_context(nc.psum_tensor(f"l0a_pt{i}", [128, 512], F32))
                  for i in range(2)]

            def tslot(dc):
                return pt[dc // 4][:, (dc % 4) * 128:(dc % 4) * 128 + 128]

            ucopy_mark = {}
            evac_mark = {}
            for t in range(NT):
                buf = t % 2
                cb1 = cols[:, buf * 16:buf * 16 + 8]
                cb2 = cols[:, buf * 16 + 8:buf * 16 + 16]
                w = ((act, ucopy_mark[t - 2]),) if t - 2 in ucopy_mark else ()
                p.op('sync', (lambda e, t=t, buf=buf: e.dma_start(
                    xstg[:, buf, :], xs[:, t, :])), waits=w, inc=(ld, 16))
                ldm = ld.n
                ln_stats(p, g, xstg[:, buf, :], sqd[:, :], cb1,
                         g.epsc[:, 0:1],
                         waits_v=((ld, ldm),), waits_s=((ld, ldm),))
                p.op('scalar', (lambda e, t=t, buf=buf, cb1=cb1: e.activation(
                    u[:, t, :], xstg[:, buf, :], AF.Identity,
                    bias=cb1[:, 1:2], scale=cb1[:, 0:1])),
                    waits=((dve, dve.n),), inc=(act, 1))
                ucopy_mark[t] = act.n
                ln_stats(p, g, u[:, t, :], sqd[:, :], cb2, g.epsc[:, 0:1],
                         waits_v=((act, ucopy_mark[t]),))
                w = [(dve, dve.n)]
                if t - 2 in evac_mark:
                    w.append((pe, evac_mark[t - 2][0]))
                p.op('scalar', (lambda e, t=t, buf=buf, cb2=cb2: e.activation(
                    nrm[:, buf, :], u[:, t, :], AF.Identity,
                    bias=cb2[:, 1:2], scale=cb2[:, 0:1])),
                    waits=tuple(w), inc=(act, 1))
                nrm_mark = act.n
                tp_marks = []
                for dc in range(ND):
                    w = [(act, nrm_mark)]
                    if t - 1 in evac_mark:
                        w.append((act, evac_mark[t - 1][1]))
                    p.op('tensor', (lambda e, buf=buf, dc=dc:
                                    e.transpose(tslot(dc),
                                                nrm[:, buf, dc * 128:dc * 128 + 128],
                                                g.ident32[:, :])),
                         waits=tuple(w), inc=(pe, 1))
                    tp_marks.append(pe.n)
                for dc in range(ND):
                    p.op('scalar', (lambda e, t=t, dc=dc: e.activation(
                        hT[:, dc, t * 128:t * 128 + 128], tslot(dc), AF.Identity,
                        bias=g.b1c[:, dc:dc + 1], scale=g.g1c[:, dc:dc + 1])),
                        waits=((pe, tp_marks[3 if dc < 4 else 7]),), inc=(act, 1))
                evac_mark[t] = (pe.n, act.n)
            p.op('tensor', lambda e: e.matmul(pt[0][:, :], g.onesr[:, :],
                                              g.bm2row[:, 0:512],
                                              start=True, stop=True),
                 waits=((act, evac_mark[NT - 1][1]),), inc=(pe, 1))
            p.op('tensor', lambda e: e.matmul(pt[1][:, :], g.onesr[:, :],
                                              g.bm2row[:, 512:1024],
                                              start=True, stop=True),
                 inc=(pe, 1))
            mk = pe.n
            p.op('vector', lambda e: e.tensor_copy(bm2b[:, 0:512], pt[0][:, :]),
                 waits=((pe, mk),))
            p.op('vector', lambda e: e.tensor_copy(bm2b[:, 512:1024], pt[1][:, :]),
                 inc=(dve, 1))
            p.emit(nc)

        if dbg == 1:
            dump_u()
            return

        TC = 256
        NCH = S // TC
        with contextlib.ExitStack() as b2:
            sb2 = lambda n, s, d=F32: b2.enter_context(nc.sbuf_tensor(n, s, d))
            ps2 = lambda n: b2.enter_context(nc.psum_tensor(n, [128, 512], F32))
            aT = sb2("l0a_aT", [128, ND, TC])
            bT = sb2("l0a_bT", [128, ND, TC])
            cT = sb2("l0a_cT", [128, ND, TC])
            am = sb2("l0a_am", [128, ND, TC])
            sc_ = sb2("l0a_sc", [128, ND, TC])
            carry = sb2("l0a_carry", [128, ND])
            pm = [ps2(f"l0a_pm{i}") for i in range(6)]

            p = Plan(g)
            p.op('gpsimd', lambda e: e.memset(carry[:, :], 0.0),
                 inc=(g.c.pool, 1))
            carry_mark = g.c.pool.n

            gate_f = {0: AF.Sigmoid, 1: AF.Tanh, 2: AF.Sigmoid}
            gate_b = {0: g.bac, 1: g.bbc, 2: g.bcc}
            dst_t = {0: aT, 1: bT, 2: cT}
            evm = {}
            scan_done = {}
            uadd_done = {}
            tpos_mark = {}
            for ch in range(NCH):
                for m in range(3):
                    for ec in range(ND):
                        bank = pm[m * 2 + (ec % 2)]
                        w = []
                        if (ch, m, ec - 2) in evm:
                            w.append((act, evm[(ch, m, ec - 2)]))
                        elif ch > 0 and (ch - 1, m, ec + ND - 2) in evm:
                            w.append((act, evm[(ch - 1, m, ec + ND - 2)]))
                        if ch == 0 and m == 0 and ec == 0:
                            w.append((cast, wabc_mark))
                        for dc in range(ND):
                            p.op('tensor', (lambda e, m=m, ec=ec, dc=dc, ch=ch,
                                            bank=bank:
                                            e.matmul(bank[:, 0:TC],
                                                     wabc[:, dc,
                                                          m * D + ec * 128:
                                                          m * D + ec * 128 + 128],
                                                     hT[:, dc, ch * TC:(ch + 1) * TC],
                                                     start=(dc == 0),
                                                     stop=(dc == ND - 1))),
                                 waits=tuple(w) if dc == 0 else (),
                                 inc=(pe, 1) if dc == ND - 1 else None)
                        mm_mark = pe.n
                        w = [(pe, mm_mark)]
                        if ch - 1 in scan_done and m < 2:
                            w.append((dve, scan_done[ch - 1]))
                        if ch - 1 in uadd_done and m == 2:
                            w.append((dve, uadd_done[ch - 1]))
                        p.op('scalar', (lambda e, m=m, ec=ec, bank=bank:
                                        e.activation(dst_t[m][:, ec, :],
                                                     bank[:, 0:TC], gate_f[m],
                                                     bias=gate_b[m][:, ec:ec + 1])),
                             waits=tuple(w), inc=(act, 1))
                        evm[(ch, m, ec)] = act.n
                abm = evm[(ch, 0, ND - 1)]
                bbm = evm[(ch, 1, ND - 1)]
                cbm = evm[(ch, 2, ND - 1)]
                flat = lambda tns: tns[:, :, :].rearrange("p c t -> p (c t)")
                p.op('vector', (lambda e: e.tensor_scalar(
                    flat(am), flat(aT), -1.0, 1.0, OP.mult, OP.add)),
                     waits=((act, abm), (g.c.pool, carry_mark)))
                p.op('vector', (lambda e: e.tensor_tensor(
                    flat(bT), flat(bT), flat(am), OP.mult)),
                     waits=((act, bbm),))
                for dc in range(ND):
                    p.op('vector', (lambda e, dc=dc: e.tensor_tensor_scan(
                        sc_[:, dc, :], aT[:, dc, :], bT[:, dc, :],
                        carry[:, dc:dc + 1], OP.mult, OP.add)))
                p.op('vector', (lambda e: e.tensor_copy(
                    carry[:, :], sc_[:, :, TC - 1:TC].rearrange("p c o -> p (c o)"))))
                p.op('vector', (lambda e: e.tensor_tensor(
                    flat(cT), flat(cT), flat(sc_), OP.mult)),
                     waits=((act, cbm),), inc=(dve, 1))
                scan_done[ch] = dve.n
                for tt in range(TC // 128):
                    t_glob = ch * (TC // 128) + tt
                    grp_mark = {}
                    for dc in range(ND):
                        bank = pm[4 + dc // 4]
                        slot = bank[:, (dc % 4) * 128:(dc % 4) * 128 + 128]
                        w = [(dve, scan_done[ch])]
                        lastdc = 3 if dc < 4 else 7
                        if (t_glob * ND + lastdc) >= 8:
                            w.append((dve, tpos_mark[t_glob * ND + lastdc - 8]))
                        p.op('tensor', (lambda e, tt=tt, dc=dc, slot=slot:
                                        e.transpose(slot,
                                                    cT[:, dc, tt * 128:tt * 128 + 128],
                                                    g.ident32[:, :])),
                             waits=tuple(w), inc=(pe, 1))
                        grp_mark[dc // 4] = pe.n
                    for dc in range(ND):
                        bank = pm[4 + dc // 4]
                        slot = bank[:, (dc % 4) * 128:(dc % 4) * 128 + 128]
                        p.op('vector', (lambda e, t_glob=t_glob, dc=dc, slot=slot:
                                        e.tensor_tensor(
                                            u[:, t_glob, dc * 128:dc * 128 + 128],
                                            u[:, t_glob, dc * 128:dc * 128 + 128],
                                            slot, OP.add)),
                             waits=((pe, grp_mark[dc // 4]),), inc=(dve, 1))
                        tpos_mark[t_glob * ND + dc] = dve.n
                uadd_done[ch] = dve.n
            p.emit(nc)
        s12.close()

        if dbg == 2:
            dump_u()
            return

        p = Plan(g)
        with contextlib.ExitStack() as b3:
            nrm = b3.enter_context(nc.sbuf_tensor("l0a_nrm3", [128, 2, D], F32))
            pt = [b3.enter_context(nc.psum_tensor(f"l0a_pt3{i}", [128, 512], F32))
                  for i in range(2)]

            def tslot(dc):
                return pt[dc // 4][:, (dc % 4) * 128:(dc % 4) * 128 + 128]

            evac_mark = {}
            nrm_mark_d = {}
            for t in range(NT):
                buf = t % 2
                cb1 = cols[:, buf * 16:buf * 16 + 8]
                ln_stats(p, g, u[:, t, :], sqd[:, :], cb1, g.epsc[:, 0:1])
                w = [(dve, dve.n)]
                if t - 2 in evac_mark:
                    w.append((pe, evac_mark[t - 2][0]))
                p.op('scalar', (lambda e, t=t, buf=buf, cb1=cb1: e.activation(
                    nrm[:, buf, :], u[:, t, :], AF.Identity,
                    bias=cb1[:, 1:2], scale=cb1[:, 0:1])),
                    waits=tuple(w), inc=(act, 1))
                nrm_mark = act.n
                nrm_mark_d[t] = nrm_mark
                p.op('vector', (lambda e, t=t: e.tensor_tensor(
                    u[:, t, :], u[:, t, :], bm2b[:, :], OP.add)),
                    waits=((act, nrm_mark),), inc=(dve, 1))
                tp_marks = []
                for dc in range(ND):
                    w = [(act, nrm_mark)]
                    if t - 1 in evac_mark:
                        w.append((act, evac_mark[t - 1][1]))
                    p.op('tensor', (lambda e, buf=buf, dc=dc:
                                    e.transpose(tslot(dc),
                                                nrm[:, buf, dc * 128:dc * 128 + 128],
                                                g.ident32[:, :])),
                         waits=tuple(w), inc=(pe, 1))
                    tp_marks.append(pe.n)
                for dc in range(ND):
                    p.op('scalar', (lambda e, t=t, dc=dc: e.activation(
                        hT[:, dc, t * 128:t * 128 + 128], tslot(dc), AF.Identity,
                        bias=g.b2c[:, dc:dc + 1], scale=g.g2c[:, dc:dc + 1])),
                        waits=((pe, tp_marks[3 if dc < 4 else 7]),), inc=(act, 1))
                evac_mark[t] = (pe.n, act.n)
            p.emit(nc)

        if dbg == 3:
            dump_u()
            return

        NSC = 8
        NHC = 4
        with contextlib.ExitStack() as b4:
            sb4 = lambda n, s, d=F32: b4.enter_context(nc.sbuf_tensor(n, s, d))
            ps4 = lambda n: b4.enter_context(nc.psum_tensor(n, [128, 512], F32))
            w1s = sb4("l0a_w1s", [128, 2, ND, 512], BF16)
            w2s = sb4("l0a_w2s", [128, 2, NHC, 1024], BF16)
            z16 = sb4("l0a_z16", [128, NHC, S], BF16)
            pz = [ps4(f"l0a_pz{i}") for i in range(4)]
            py = [ps4(f"l0a_py{i}") for i in range(4)]

            p = Plan(g)
            w1m, w2m = {}, {}
            mm1_done = {}
            mm2_done = {}
            uadd_mark = {}
            gelu_mark = {}
            for sc in range(NSC):
                buf = sc % 2
                w = []
                if sc - 2 in mm1_done:
                    w.append((pe, mm1_done[sc - 2]))
                p.op('gpsimd', (lambda e, sc=sc, buf=buf: e.dma_start(
                    w1s[:, buf, :, :],
                    dt["W1"].rearrange("(c p) h -> p c h", p=128)
                    [:, :, sc * 512:(sc + 1) * 512])),
                    waits=tuple(w), inc=(cast, 16))
                w1m[sc] = cast.n
                w = []
                if sc - 2 in mm2_done:
                    w.append((pe, mm2_done[sc - 2]))
                p.op('gpsimd', (lambda e, sc=sc, buf=buf: e.dma_start(
                    w2s[:, buf, :, :],
                    dt["W2"].rearrange("(c p) d -> p c d", p=128)
                    [:, sc * NHC:(sc + 1) * NHC, :])),
                    waits=tuple(w), inc=(cast, 16))
                w2m[sc] = cast.n

                for hc in range(NHC):
                    for dc in range(ND):
                        for tg in range(4):
                            w = []
                            if dc == 0 and tg == 0:
                                w = [(cast, w1m[sc])]
                                if hc == 0 and sc > 0:
                                    w.append((pe, mm2_done.get(sc - 1, 0)))
                                if (sc, hc - 1) in gelu_mark:
                                    w.append((act, gelu_mark[(sc, hc - 1)]))
                                elif sc > 0 and (sc - 1, NHC - 1) in gelu_mark:
                                    w.append((act, gelu_mark[(sc - 1, NHC - 1)]))
                            p.op('tensor', (lambda e, sc=sc, hc=hc, dc=dc, tg=tg,
                                            buf=buf: e.matmul(
                                                pz[tg][:, :],
                                                w1s[:, buf, dc, hc * 128:hc * 128 + 128],
                                                hT[:, dc, tg * 512:(tg + 1) * 512],
                                                start=(dc == 0), stop=(dc == ND - 1))),
                                 waits=tuple(w),
                                 inc=(pe, 1) if (dc == ND - 1 and tg == 3) else None)
                    hcm = pe.n
                    for tg in range(4):
                        gw = [(pe, hcm)]
                        if hc == 0 and tg == 0 and sc - 1 in mm2_done:
                            gw.append((pe, mm2_done[sc - 1]))
                        p.op('scalar', (lambda e, sc=sc, hc=hc, tg=tg: e.activation(
                            z16[:, hc, tg * 512:(tg + 1) * 512], pz[tg][:, :],
                            AF.Gelu,
                            bias=g.bm1c[:, sc * NHC + hc:sc * NHC + hc + 1])),
                            waits=tuple(gw), inc=(act, 1))
                    gelu_mark[(sc, hc)] = act.n
                mm1_done[sc] = pe.n

                for tt in range(NT):
                    for dh in range(2):
                        bank = py[(tt % 2) * 2 + dh]
                        w = []
                        if tt == 0 and dh == 0:
                            w.append((cast, w2m[sc]))
                            w.append((act, gelu_mark[(sc, NHC - 1)]))
                        key = (sc, tt - 2, dh)
                        if key in uadd_mark:
                            w.append((dve, uadd_mark[key]))
                        elif sc > 0 and (sc - 1, NT - 2 + tt, dh) in uadd_mark:
                            w.append((dve, uadd_mark[(sc - 1, NT - 2 + tt, dh)]))
                        for hc in range(NHC):
                            p.op('tensor', (lambda e, sc=sc, tt=tt, dh=dh, hc=hc,
                                            buf=buf, bank=bank: e.matmul(
                                                bank[:, :],
                                                z16[:, hc, tt * 128:tt * 128 + 128],
                                                w2s[:, buf, hc, dh * 512:(dh + 1) * 512],
                                                start=(hc == 0), stop=(hc == NHC - 1))),
                                 waits=tuple(w) if hc == 0 else (),
                                 inc=(pe, 1) if hc == NHC - 1 else None)
                        mmm = pe.n
                        p.op('vector', (lambda e, tt=tt, dh=dh, bank=bank:
                                        e.tensor_tensor(
                                            u[:, tt, dh * 512:(dh + 1) * 512],
                                            u[:, tt, dh * 512:(dh + 1) * 512],
                                            bank[:, :], OP.add)),
                             waits=((pe, mmm),), inc=(dve, 1))
                        uadd_mark[(sc, tt, dh)] = dve.n
                mm2_done[sc] = pe.n
            p.emit(nc)

        if dbg == 4:
            dump_u()
            return

        b5 = ph.enter_context(contextlib.ExitStack())
        xstg = b5.enter_context(nc.sbuf_tensor("l0a_xstg5", [128, 2, D], F32))
        nrm = b5.enter_context(nc.sbuf_tensor("l0a_nrm5", [128, 2, D], F32))
        p = Plan(g)
        omark = {}
        for t in range(NT):
            buf = t % 2
            cb1 = cols[:, buf * 16:buf * 16 + 8]
            w = ((st, omark[t - 2]),) if t - 2 in omark else ()
            p.op('sync', (lambda e, t=t, buf=buf: e.dma_start(
                xstg[:, buf, :], xs[:, t, :])), waits=w, inc=(ld, 16))
            ldm = ld.n
            ln_stats(p, g, u[:, t, :], sqd[:, :], cb1, g.epsc[:, 0:1])
            wv = [(ld, ldm), (dve, dve.n)]
            if t - 2 in omark:
                wv.append((st, omark[t - 2]))
            p.op('vector', (lambda e, buf=buf, cb1=cb1: e.tensor_scalar(
                nrm[:, buf, :], xstg[:, buf, :], cb1[:, 1:2], None,
                OP.add)), waits=tuple(wv))
            p.op('vector', (lambda e, t=t, buf=buf, cb1=cb1:
                            e.scalar_tensor_tensor(
                nrm[:, buf, :], u[:, t, :], cb1[:, 0:1], nrm[:, buf, :],
                OP.mult, OP.add)), inc=(dve, 1))
            vm = dve.n
            p.op('sync', (lambda e, t=t, buf=buf: e.dma_start(
                xd[:, t, :], nrm[:, buf, :])), waits=((dve, vm),), inc=(st, 16))
            omark[t] = st.n
        need = st.n
        p.op('sync', lambda e: e.nop(), waits=((st, need),))
        p.emit(nc)



def phase_l0f(g, x_src, x_dst, dbg=None):
    nc = g.nc
    dt = g.dt
    c = g.c
    ld, cast, act, dve, pe, st = c.ld, c.cast, c.act, c.dve, c.pe, c.st
    xs = x_src.rearrange("(t p) d -> p t d", p=128)
    xd = x_dst.rearrange("(t p) d -> p t d", p=128)
    NE = E0R + E0S

    with contextlib.ExitStack() as ph:
        sbuf = lambda n, s, d=F32: ph.enter_context(nc.sbuf_tensor(n, s, d))
        xhatT = sbuf("l0f_xhatT", [128, ND, S], BF16)
        y = sbuf("l0f_y", [128, NT, D])
        mw = sbuf("l0f_mw", [128, NT * 16])
        sqd = sbuf("l0f_sqd", [128, D])
        cols = sbuf("l0f_cols", [128, 32])

        with contextlib.ExitStack() as b0:
            sb0 = lambda n, s, d=F32: b0.enter_context(nc.sbuf_tensor(n, s, d))
            xstg = sb0("l0f_xstg", [128, 2, D])
            nrm = sb0("l0f_nrm", [128, 2, D])
            rwc = sb0("l0f_rwc", [128, ND, E0R])
            xf32 = sb0("l0f_xf32", [128, D])
            lg = sb0("l0f_lg", [128, 2 * 16])
            prb = sb0("l0f_prb", [128, 2 * 16])
            pt = [b0.enter_context(nc.psum_tensor(f"l0f_pt{i}", [128, 512], F32))
                  for i in range(2)]
            prt = b0.enter_context(nc.psum_tensor("l0f_prt", [128, 16], F32))

            def tslot(dc):
                return pt[dc // 4][:, (dc % 4) * 128:(dc % 4) * 128 + 128]

            p = Plan(g)
            p.op('sync', lambda e: e.dma_start(
                rwc[:, :, :], dt["rw0_pc"][:, :, :]), inc=(ld, 16))
            rw_mark = ld.n

            evac_mark = {}
            lg_mark = {}
            for t in range(NT):
                buf = t % 2
                cb1 = cols[:, buf * 16:buf * 16 + 8]
                cb2 = cols[:, buf * 16 + 8:buf * 16 + 16]
                w = ((act, evac_mark[t - 2][2]),) if t - 2 in evac_mark else ()
                p.op('sync', (lambda e, t=t, buf=buf: e.dma_start(
                    xstg[:, buf, :], xs[:, t, :])), waits=w, inc=(ld, 16))
                ldm = ld.n
                ln_stats(p, g, xstg[:, buf, :], sqd[:, :], cb1, g.epsc[:, 0:1],
                         waits_v=((ld, ldm),), waits_s=((ld, ldm),))
                w = [(dve, dve.n)]
                if t - 2 in evac_mark:
                    w.append((pe, evac_mark[t - 2][0]))
                p.op('scalar', (lambda e, buf=buf, cb1=cb1: e.activation(
                    nrm[:, buf, :], xstg[:, buf, :], AF.Identity,
                    bias=cb1[:, 1:2], scale=cb1[:, 0:1])), waits=tuple(w))
                xn_mark = act.n
                p.op('scalar', (lambda e, buf=buf, cb2=cb2: e.activation(
                    sqd[:, :], nrm[:, buf, :], AF.Square,
                    accum_out=cb2[:, 2:3])))
                mka = act.n
                p.op('vector', (lambda e, cb2=cb2: e.tensor_scalar(
                    cb2[:, 3:4], cb2[:, 2:3], 1.0 / D, None, OP.mult)),
                    waits=((act, mka),))
                p.op('scalar', (lambda e, cb2=cb2: e.activation(
                    cb2[:, 4:5], cb2[:, 3:4], AF.Sqrt, bias=g.epsc[:, 1:2])),
                    waits=((dve, dve.n),))
                p.op('vector', (lambda e, cb2=cb2: e.reciprocal(
                    cb2[:, 0:1], cb2[:, 4:5])), waits=((act, act.n),),
                    inc=(dve, 1))
                p.op('vector', (lambda e, buf=buf, cb2=cb2: e.tensor_scalar(
                    nrm[:, buf, :], nrm[:, buf, :], cb2[:, 0:1], None,
                    OP.mult)), inc=(dve, 1))
                xhat_mark = dve.n
                tp_marks = []
                for dc in range(ND):
                    w = [(dve, xhat_mark)]
                    if t - 1 in evac_mark:
                        w.append((act, evac_mark[t - 1][2]))
                    p.op('tensor', (lambda e, buf=buf, dc=dc:
                                    e.transpose(tslot(dc),
                                                nrm[:, buf, dc * 128:dc * 128 + 128],
                                                g.ident32[:, :])),
                         waits=tuple(w), inc=(pe, 1))
                    tp_marks.append(pe.n)
                rtp = rt_mark_prev = locals().get("rt_mark")
                for dc in range(ND):
                    p.op('scalar', (lambda e, t=t, dc=dc: e.activation(
                        xhatT[:, dc, t * 128:t * 128 + 128], tslot(dc),
                        AF.Identity)),
                        waits=((pe, tp_marks[3 if dc < 4 else 7]),))
                for dc in range(ND):
                    w = [(pe, tp_marks[3 if dc < 4 else 7])]
                    if dc == 0 and rtp is not None:
                        w.append((pe, rtp))
                    p.op('scalar', (lambda e, dc=dc: e.activation(
                        xf32[:, dc * 128:dc * 128 + 128], tslot(dc),
                        AF.Identity)), waits=tuple(w))
                ev2 = act.n
                w = [(act, ev2), (ld, rw_mark)]
                if t - 1 in lg_mark:
                    w.append((dve, lg_mark[t - 1]))
                for dc in range(ND):
                    p.op('tensor', (lambda e, dc=dc: e.matmul(
                        prt[:, 0:E0R], xf32[:, dc * 128:dc * 128 + 128],
                        rwc[:, dc, :],
                        start=(dc == 0), stop=(dc == ND - 1))),
                        waits=tuple(w) if dc == 0 else (),
                        inc=(pe, 1) if dc == ND - 1 else None)
                rt_mark = pe.n
                evac_mark[t] = (pe.n, None, ev2)
                p.op('vector', (lambda e, buf=buf, cb2=cb2: e.tensor_scalar(
                    lg[:, buf * 16:buf * 16 + E0R], prt[:, 0:E0R],
                    cb2[:, 4:5], None, OP.mult)), waits=((pe, rt_mark),),
                    inc=(dve, 1))
                lg_mark[t] = dve.n
                L = lg[:, buf * 16:buf * 16 + E0R]
                P_ = prb[:, buf * 16:buf * 16 + E0R]
                c5, c6, c7 = cb2[:, 5:6], cb2[:, 6:7], cb2[:, 7:8]
                p.op('vector', (lambda e, L=L, c5=c5: e.reduce_max(
                    c5, L, axis=AX, negate=True)))
                p.op('scalar', (lambda e, L=L, P_=P_, c5=c5, c6=c6:
                                e.activation(P_, L, AF.Exp, bias=c5,
                                             accum_out=c6)),
                    waits=((dve, dve.n),))
                p.op('vector', (lambda e, c6=c6: e.reciprocal(c6, c6)),
                     waits=((act, act.n),))
                p.op('vector', (lambda e, P_=P_, c6=c6: e.tensor_scalar(
                    P_, P_, c6, None, OP.mult)))
                p.op('vector', (lambda e, P_=P_, c5=c5: e.reduce_max(
                    c5, P_, axis=AX, negate=True)))
                p.op('vector', (lambda e, L=L, P_=P_, c5=c5: e.tensor_scalar(
                    L, P_, c5, 0.0, OP.add, OP.is_ge)))
                p.op('vector', (lambda e, L=L, P_=P_: e.scalar_tensor_tensor(
                    L, L, NEG, P_, OP.mult, OP.add)))
                p.op('vector', (lambda e, L=L, c7=c7: e.reduce_max(
                    c7, L, axis=AX, negate=True)))
                p.op('vector', (lambda e, L=L, P_=P_, c7=c7: e.tensor_scalar(
                    L, P_, c7, 0.0, OP.add, OP.is_ge)))
                p.op('vector', (lambda e, L=L, P_=P_: e.tensor_mul(P_, P_, L)))
                p.op('vector', (lambda e, P_=P_, c7=c7: e.reduce_sum(
                    c7, P_, axis=AX)))
                p.op('vector', (lambda e, c7=c7: e.tensor_scalar(
                    c7, c7, 1e-9, None, OP.add)))
                p.op('vector', (lambda e, c7=c7: e.reciprocal(c7, c7)))
                p.op('vector', (lambda e, t=t, P_=P_, c7=c7: e.tensor_scalar(
                    mw[:, t * 16:t * 16 + E0R], P_, c7, None, OP.mult)),
                    inc=(dve, 1))
                lg_mark[t] = dve.n
            p.emit(nc)

        with contextlib.ExitStack() as b1:
            sb1 = lambda n, s, d=F32: b1.enter_context(nc.sbuf_tensor(n, s, d))
            w1p = sb1("l0f_w1p", [128, ND, H_FF0], BF16)
            w2p = sb1("l0f_w2p", [128, H_FF0 // 128, D], BF16)
            zb = sb1("l0f_zb", [128, H_FF0 // 128, 512], BF16)
            stage = sb1("l0f_stage", [128, 2, H_FF0])
            pz = [b1.enter_context(nc.psum_tensor(f"l0f_pz{i}", [128, 512], F32))
                  for i in range(2)]
            py = [b1.enter_context(nc.psum_tensor(f"l0f_py{i}", [128, 512], F32))
                  for i in range(4)]
            NHC2 = H_FF0 // 128

            p = Plan(g)
            mm1_done = {}
            mm2_done = {}
            mm2_tg_done = {}
            silu_mark = {}
            yev_mark = {}
            w1sc_mark = {}
            w2_mark = {}
            for e in range(NE):
                shared = e >= E0R
                if shared:
                    w1src = dt["f0w1s"][e - E0R]
                    w2src = dt["f0w2s"][e - E0R]
                else:
                    w1src = dt["f0w1"][e]
                    w2src = dt["f0w2"][e]
                for dc in range(ND):
                    w = []
                    if (e, dc - 2) in w1sc_mark:
                        w.append((dve, w1sc_mark[(e, dc - 2)]))
                    elif (e - 1, dc + ND - 2) in w1sc_mark:
                        w.append((dve, w1sc_mark[(e - 1, dc + ND - 2)]))
                    p.op('sync', (lambda e_, dc=dc, w1src=w1src: e_.dma_start(
                        stage[:, dc % 2, :],
                        w1src[dc * 128:(dc + 1) * 128, :])),
                        waits=tuple(w), inc=(ld, 16))
                    ldm = ld.n
                    w = [(ld, ldm)]
                    if dc == 0 and e - 1 in mm1_done:
                        w.append((pe, mm1_done[e - 1]))
                    p.op('vector', (lambda e_, e=e, dc=dc: e_.tensor_scalar(
                        w1p[:, dc, :], stage[:, dc % 2, :],
                        g.rmsc[:, e * ND + dc:e * ND + dc + 1], None,
                        OP.mult)), waits=tuple(w), inc=(dve, 1))
                    w1sc_mark[(e, dc)] = dve.n
                w = []
                if e - 1 in mm2_done:
                    w.append((pe, mm2_done[e - 1]))
                p.op('gpsimd', (lambda e_, w2src=w2src: e_.dma_start(
                    w2p[:, :, :],
                    w2src.rearrange("(c p) d -> p c d", p=128))),
                    waits=tuple(w), inc=(cast, 16))
                w2_mark[e] = cast.n

                for tg in range(4):
                    for hc in range(NHC2):
                        for dc in range(ND):
                            w = []
                            if dc == 0:
                                if hc == 0 and tg == 0:
                                    w.append((dve, w1sc_mark[(e, ND - 1)]))
                                if (e, tg, hc - 2) in silu_mark:
                                    w.append((act, silu_mark[(e, tg, hc - 2)]))
                                elif (e, tg - 1, NHC2 - 2 + hc) in silu_mark:
                                    w.append((act,
                                              silu_mark[(e, tg - 1,
                                                         NHC2 - 2 + hc)]))
                                elif e > 0 and (e - 1, 3, NHC2 - 2 + hc) in silu_mark:
                                    w.append((act,
                                              silu_mark[(e - 1, 3,
                                                         NHC2 - 2 + hc)]))
                            p.op('tensor', (lambda e_, hc=hc, dc=dc, tg=tg:
                                            e_.matmul(
                                                pz[hc % 2][:, :],
                                                w1p[:, dc, hc * 128:hc * 128 + 128],
                                                xhatT[:, dc, tg * 512:(tg + 1) * 512],
                                                start=(dc == 0),
                                                stop=(dc == ND - 1))),
                                 waits=tuple(w),
                                 inc=(pe, 1) if dc == ND - 1 else None)
                        mmm = pe.n
                        w = [(pe, mmm)]
                        if hc == 0:
                            key = (e, tg - 1) if tg > 0 else (e - 1, 3)
                            if key in mm2_tg_done:
                                w.append((pe, mm2_tg_done[key]))
                        p.op('scalar', (lambda e_, hc=hc: e_.activation(
                            zb[:, hc, :], pz[hc % 2][:, :], AF.Silu)),
                            waits=tuple(w))
                        silu_mark[(e, tg, hc)] = act.n
                    mm1_last = pe.n
                    for tt in range(4):
                        t_glob = tg * 4 + tt
                        for dh in range(2):
                            bank = py[(tt % 2) * 2 + dh]
                            w = []
                            if tt == 0 and dh == 0:
                                w.append((act, silu_mark[(e, tg, NHC2 - 1)]))
                                w.append((cast, w2_mark[e]))
                            key = (e, tg, tt - 2, dh)
                            if key in yev_mark:
                                w.append((dve, yev_mark[key]))
                            elif tg > 0 and (e, tg - 1, 2 + tt, dh) in yev_mark:
                                w.append((dve, yev_mark[(e, tg - 1, 2 + tt, dh)]))
                            elif e > 0 and (e - 1, 3, 2 + tt, dh) in yev_mark:
                                w.append((dve, yev_mark[(e - 1, 3, 2 + tt, dh)]))
                            for hc in range(NHC2):
                                p.op('tensor', (lambda e_, hc=hc, tt=tt, dh=dh,
                                                bank=bank: e_.matmul(
                                                    bank[:, :],
                                                    zb[:, hc, tt * 128:tt * 128 + 128],
                                                    w2p[:, hc, dh * 512:(dh + 1) * 512],
                                                    start=(hc == 0),
                                                    stop=(hc == NHC2 - 1))),
                                     waits=tuple(w) if hc == 0 else (),
                                     inc=(pe, 1) if hc == NHC2 - 1 else None)
                            mmm = pe.n
                            if e == 0:
                                p.op('vector', (lambda e_, t_glob=t_glob, dh=dh,
                                                bank=bank: e_.tensor_scalar(
                                                    y[:, t_glob, dh * 512:(dh + 1) * 512],
                                                    bank[:, :],
                                                    mw[:, t_glob * 16:t_glob * 16 + 1],
                                                    None, OP.mult)),
                                     waits=((pe, mmm),), inc=(dve, 1))
                            else:
                                mcol = (1.0 if shared else
                                        mw[:, t_glob * 16 + e:t_glob * 16 + e + 1])
                                p.op('vector', (lambda e_, t_glob=t_glob, dh=dh,
                                                bank=bank, mcol=mcol:
                                                e_.scalar_tensor_tensor(
                                                    y[:, t_glob, dh * 512:(dh + 1) * 512],
                                                    bank[:, :], mcol,
                                                    y[:, t_glob, dh * 512:(dh + 1) * 512],
                                                    OP.mult, OP.add)),
                                     waits=((pe, mmm),), inc=(dve, 1))
                            yev_mark[(e, tg, tt, dh)] = dve.n
                    mm2_tg_done[(e, tg)] = pe.n
                    if tg == 3:
                        mm1_done[e] = mm1_last
                mm2_done[e] = pe.n
            p.emit(nc)

        with contextlib.ExitStack() as b5:
            xstg = b5.enter_context(nc.sbuf_tensor("l0f_xstg5", [128, 2, D], F32))
            nrm = b5.enter_context(nc.sbuf_tensor("l0f_nrm5", [128, 2, D], F32))
            p = Plan(g)
            omark = {}
            for t in range(NT):
                buf = t % 2
                cb1 = cols[:, buf * 16:buf * 16 + 8]
                w = ((st, omark[t - 2]),) if t - 2 in omark else ()
                p.op('sync', (lambda e, t=t, buf=buf: e.dma_start(
                    xstg[:, buf, :], xs[:, t, :])), waits=w, inc=(ld, 16))
                ldm = ld.n
                ln_stats(p, g, y[:, t, :], sqd[:, :], cb1, g.epsc[:, 0:1])
                wv = [(ld, ldm), (dve, dve.n)]
                if t - 2 in omark:
                    wv.append((st, omark[t - 2]))
                p.op('vector', (lambda e, buf=buf, cb1=cb1: e.tensor_scalar(
                    nrm[:, buf, :], xstg[:, buf, :], cb1[:, 1:2], None,
                    OP.add)), waits=tuple(wv))
                p.op('vector', (lambda e, t=t, buf=buf, cb1=cb1:
                                e.scalar_tensor_tensor(
                    nrm[:, buf, :], y[:, t, :], cb1[:, 0:1], nrm[:, buf, :],
                    OP.mult, OP.add)), inc=(dve, 1))
                vm = dve.n
                p.op('sync', (lambda e, t=t, buf=buf: e.dma_start(
                    xd[:, t, :], nrm[:, buf, :])), waits=((dve, vm),),
                    inc=(st, 16))
                omark[t] = st.n
            need = st.n
            p.op('sync', lambda e: e.nop(), waits=((st, need),))
            p.emit(nc)




def phase_l1a(g, x_src, x_dst, dbg=None):
    nc = g.nc
    dt = g.dt
    c = g.c
    ld, cast, act, dve, pe, st = c.ld, c.cast, c.act, c.dve, c.pe, c.st
    xs = x_src.rearrange("(t p) d -> p t d", p=128)
    xd = x_dst.rearrange("(t p) d -> p t d", p=128)

    with contextlib.ExitStack() as ph:
        qT = ph.enter_context(nc.sbuf_tensor("l1a_qT", [128, ND, S], BF16))
        kT = ph.enter_context(nc.sbuf_tensor("l1a_kT", [128, ND, S], BF16))
        vtm = ph.enter_context(nc.sbuf_tensor("l1a_v", [128, NT, D], BF16))
        sqd = ph.enter_context(nc.sbuf_tensor("l1a_sqd", [128, D], F32))
        cols = ph.enter_context(nc.sbuf_tensor("l1a_cols", [128, 32], F32))
        xnT_st = ph.enter_context(contextlib.ExitStack())
        xn3T = xnT_st.enter_context(nc.sbuf_tensor("l1a_xnT", [128, ND, S], BF16))

        with contextlib.ExitStack() as b0:
            xstg = b0.enter_context(nc.sbuf_tensor("l1a_xstg", [128, 2, D], F32))
            nrm = b0.enter_context(nc.sbuf_tensor("l1a_nrm", [128, 2, D], F32))
            pt = [b0.enter_context(nc.psum_tensor(f"l1a_pt{i}", [128, 512], F32))
                  for i in range(2)]

            def tslot(dc):
                return pt[dc // 4][:, (dc % 4) * 128:(dc % 4) * 128 + 128]

            p = Plan(g)
            tp_mark = {}
            ev_mark = {}
            for t in range(NT):
                buf = t % 2
                cb1 = cols[:, buf * 16:buf * 16 + 8]
                w = ((act, ev_mark[t - 2]),) if t - 2 in ev_mark else ()
                p.op('sync', (lambda e, t=t, buf=buf: e.dma_start(
                    xstg[:, buf, :], xs[:, t, :])), waits=w, inc=(ld, 16))
                ldm = ld.n
                ln_stats(p, g, xstg[:, buf, :], sqd[:, :], cb1, g.epsc[:, 0:1],
                         waits_v=((ld, ldm),), waits_s=((ld, ldm),))
                w = [(dve, dve.n)]
                if t - 2 in tp_mark:
                    w.append((pe, tp_mark[t - 2]))
                p.op('scalar', (lambda e, buf=buf, cb1=cb1: e.activation(
                    nrm[:, buf, :], xstg[:, buf, :], AF.Identity,
                    bias=cb1[:, 1:2], scale=cb1[:, 0:1])), waits=tuple(w))
                nrm_mark = act.n
                tpg = []
                for dc in range(ND):
                    w = [(act, nrm_mark)]
                    if t - 1 in ev_mark:
                        w.append((act, ev_mark[t - 1]))
                    p.op('tensor', (lambda e, buf=buf, dc=dc:
                                    e.transpose(tslot(dc),
                                                nrm[:, buf, dc * 128:dc * 128 + 128],
                                                g.ident32[:, :])),
                         waits=tuple(w), inc=(pe, 1))
                    tpg.append(pe.n)
                tp_mark[t] = pe.n
                for dc in range(ND):
                    p.op('scalar', (lambda e, t=t, dc=dc: e.activation(
                        xn3T[:, dc, t * 128:t * 128 + 128], tslot(dc),
                        AF.Identity)),
                        waits=((pe, tpg[3 if dc < 4 else 7]),))
                ev_mark[t] = act.n
            p.emit(nc)

        with contextlib.ExitStack() as b1:
            wn = b1.enter_context(nc.sbuf_tensor("l1a_wn", [128, ND, D], BF16))
            wt = b1.enter_context(nc.sbuf_tensor("l1a_wt", [128, ND, D], BF16))
            ptb = [b1.enter_context(nc.psum_tensor(f"l1a_ptb{i}", [128, 1024],
                                                   BF16)) for i in range(2)]
            pp = [b1.enter_context(nc.psum_tensor(f"l1a_pp{i}", [128, 512], F32))
                  for i in range(4)]
            p = Plan(g)
            for wi, (wname, scale) in enumerate(
                    (("Wq", 1.0 / 32.0), ("Wk", None), ("Wv", None))):
                w = []
                if wi > 0:
                    w.append((pe, proj_pe_done))
                p.op('gpsimd', (lambda e, wname=wname: e.dma_start(
                    wn[:, :, :],
                    dt[wname].rearrange("(c p) h -> p c h", p=128))),
                    waits=tuple(w), inc=(cast, 16))
                ld_mark = cast.n
                grp_ev = {}
                for gi in range(8):
                    bank = ptb[gi % 2]
                    w = [(cast, ld_mark)]
                    if gi - 2 in grp_ev:
                        w.append((dve, grp_ev[gi - 2]))
                    for j in range(8):
                        idx = gi * 8 + j
                        oc, hc = idx // 8, idx % 8
                        p.op('tensor', (lambda e, oc=oc, hc=hc, bank=bank, j=j:
                                        e.transpose(
                                            bank[:, j * 128:j * 128 + 128],
                                            wn[:, oc, hc * 128:hc * 128 + 128],
                                            g.ident16[:, :])),
                             waits=tuple(w) if j == 0 else (), inc=(pe, 1))
                    gm = pe.n
                    for j in range(8):
                        idx = gi * 8 + j
                        oc, hc = idx // 8, idx % 8
                        p.op('vector', (lambda e, oc=oc, hc=hc, bank=bank, j=j:
                                        e.tensor_copy(
                                            wt[:, hc, oc * 128:oc * 128 + 128],
                                            bank[:, j * 128:j * 128 + 128])),
                             waits=((pe, gm),), inc=(dve, 1))
                    grp_ev[gi] = dve.n
                wt_mark = dve.n
                ev_d = {}
                if wi < 2:
                    dst = qT if wi == 0 else kT
                    for oc in range(ND):
                        for sg in range(4):
                            gidx = oc * 4 + sg
                            bank = pp[gidx % 2]
                            w = []
                            if gidx == 0:
                                w.append((dve, wt_mark))
                            if gidx - 2 in ev_d:
                                w.append((act, ev_d[gidx - 2]))
                            for hc in range(ND):
                                p.op('tensor', (lambda e, oc=oc, sg=sg, hc=hc,
                                                bank=bank: e.matmul(
                                                    bank[:, :],
                                                    wt[:, hc, oc * 128:oc * 128 + 128],
                                                    xn3T[:, hc, sg * 512:(sg + 1) * 512],
                                                    start=(hc == 0),
                                                    stop=(hc == ND - 1))),
                                     waits=tuple(w) if hc == 0 else (),
                                     inc=(pe, 1) if hc == ND - 1 else None)
                            mmm = pe.n
                            kw = {"scale": scale} if scale is not None else {}
                            p.op('scalar', (lambda e, dst=dst, oc=oc, sg=sg,
                                            bank=bank, kw=kw: e.activation(
                                                dst[:, oc, sg * 512:(sg + 1) * 512],
                                                bank[:, :], AF.Identity, **kw)),
                                 waits=((pe, mmm),))
                            ev_d[gidx] = act.n
                else:
                    for ti in range(NT):
                        for dh in range(2):
                            gidx = ti * 2 + dh
                            bank = pp[2 + gidx % 2]
                            w = []
                            if gidx == 0:
                                w.append((dve, wt_mark))
                            if gidx - 2 in ev_d:
                                w.append((act, ev_d[gidx - 2]))
                            for hc in range(ND):
                                p.op('tensor', (lambda e, ti=ti, dh=dh, hc=hc,
                                                bank=bank: e.matmul(
                                                    bank[:, :],
                                                    xn3T[:, hc, ti * 128:ti * 128 + 128],
                                                    wt[:, hc, dh * 512:(dh + 1) * 512],
                                                    start=(hc == 0),
                                                    stop=(hc == ND - 1))),
                                     waits=tuple(w) if hc == 0 else (),
                                     inc=(pe, 1) if hc == ND - 1 else None)
                            mmm = pe.n
                            p.op('scalar', (lambda e, ti=ti, dh=dh, bank=bank:
                                            e.activation(
                                                vtm[:, ti, dh * 512:(dh + 1) * 512],
                                                bank[:, :], AF.Identity)),
                                 waits=((pe, mmm),))
                            ev_d[gidx] = act.n
                proj_pe_done = pe.n
            p.emit(nc)
        xnT_st.close()

        with contextlib.ExitStack() as b2:
            sb2 = lambda n, s, d=F32: b2.enter_context(nc.sbuf_tensor(n, s, d))
            srow = sb2("l1a_srow", [128, 2, S])
            arow = sb2("l1a_arow", [128, 2, S], BF16)
            atile = sb2("l1a_atile", [128, 8, 128], BF16)
            ytile = sb2("l1a_ytile", [128, 2, D])
            xstg = sb2("l1a_xstg2", [128, 2, D])
            rcol = sb2("l1a_rcol", [128, 4])
            psc = [b2.enter_context(nc.psum_tensor(f"l1a_psc{i}", [128, 512], F32))
                   for i in range(2)]
            pta = [b2.enter_context(nc.psum_tensor(f"l1a_pta{i}", [128, 1024],
                                                   BF16)) for i in range(2)]
            pyb = [b2.enter_context(nc.psum_tensor(f"l1a_py{i}", [128, 512], F32))
                   for i in range(4)]

            p = Plan(g)
            bank_last_sc = {}
            soft_mark = {}
            tp_last = {}
            av_done = {}
            yev_mark = {}
            at_grp_ev = {}
            out_mark = {}

            def emit_scores(i):
                buf = i % 2
                nkb = i // 4 + 1
                for kb in range(nkb):
                    N = 512 if kb < i // 4 else (i % 4 + 1) * 128
                    bank = psc[kb % 2]
                    w = []
                    if (kb % 2) in bank_last_sc:
                        w.append((dve, bank_last_sc[kb % 2]))
                    if kb == 0 and i - 2 in soft_mark:
                        w.append((act, soft_mark[i - 2][0]))
                    for oc in range(ND):
                        p.op('tensor', (lambda e, i=i, kb=kb, oc=oc, bank=bank,
                                        N=N: e.matmul(
                                            bank[:, 0:N],
                                            qT[:, oc, i * 128:i * 128 + 128],
                                            kT[:, oc, kb * 512:kb * 512 + N],
                                            start=(oc == 0), stop=(oc == ND - 1))),
                             waits=tuple(w) if oc == 0 else (),
                             inc=(pe, 1) if oc == ND - 1 else None)
                    mmm = pe.n
                    wv = [(pe, mmm)]
                    if kb == 0 and i - 2 in soft_mark:
                        wv.append((act, soft_mark[i - 2][0]))
                    p.op('vector', (lambda e, i=i, kb=kb, buf=buf, bank=bank,
                                    N=N: e.tensor_copy(
                                        srow[:, buf, kb * 512:kb * 512 + N],
                                        bank[:, 0:N])),
                         waits=tuple(wv), inc=(dve, 1))
                    bank_last_sc[kb % 2] = dve.n
                W_ = (i + 1) * 128
                cb = cols[:, (i % 2) * 16:(i % 2) * 16 + 8]
                p.op('vector', (lambda e, i=i, buf=buf: e.tensor_tensor(
                    srow[:, buf, i * 128:i * 128 + 128],
                    srow[:, buf, i * 128:i * 128 + 128],
                    g.causal32[:, :], OP.add)))
                p.op('vector', (lambda e, buf=buf, W_=W_, cb=cb: e.reduce_max(
                    cb[:, 0:1], srow[:, buf, 0:W_], axis=AX, negate=True)))
                w = [(dve, dve.n)]
                if i - 2 in tp_last:
                    w.append((pe, tp_last[i - 2]))
                p.op('scalar', (lambda e, buf=buf, W_=W_, cb=cb: e.activation(
                    arow[:, buf, 0:W_], srow[:, buf, 0:W_], AF.Exp,
                    bias=cb[:, 0:1], accum_out=cb[:, 1:2])), waits=tuple(w))
                em = act.n
                p.op('vector', (lambda e, i=i, cb=cb: e.reciprocal(
                    rcol[:, i % 2:i % 2 + 1], cb[:, 1:2])),
                    waits=((act, em),), inc=(dve, 1))
                soft_mark[i] = (em, dve.n)

            def emit_av(i):
                buf = i % 2
                ngrp = (i + 1 + 7) // 8
                for grp in range(ngrp):
                    k0 = grp * 8
                    k1 = min(k0 + 8, i + 1)
                    bank = pta[grp % 2]
                    w = [(act, soft_mark[i][0])]
                    if (i, grp - 2) in at_grp_ev:
                        w.append((dve, at_grp_ev[(i, grp - 2)]))
                    elif i - 1 in tp_last:
                        prev_ngrp = (i + 7) // 8
                        pg = prev_ngrp - (2 - grp % 2)
                        while pg >= 0 and (i - 1, pg) not in at_grp_ev:
                            pg -= 1
                        w.append((dve, at_evall[i - 1]))
                    for kc in range(k0, k1):
                        p.op('tensor', (lambda e, buf=buf, kc=kc, bank=bank,
                                        k0=k0: e.transpose(
                                            bank[:, (kc - k0) * 128:(kc - k0) * 128 + 128],
                                            arow[:, buf, kc * 128:kc * 128 + 128],
                                            g.ident16[:, :])),
                             waits=tuple(w) if kc == k0 else (), inc=(pe, 1))
                    gm = pe.n
                    if grp == ngrp - 1:
                        tp_last[i] = gm
                    for kc in range(k0, k1):
                        p.op('vector', (lambda e, kc=kc, bank=bank, k0=k0:
                                        e.tensor_copy(
                                            atile[:, kc % 8, :],
                                            bank[:, (kc - k0) * 128:(kc - k0) * 128 + 128])),
                             waits=((pe, gm),), inc=(dve, 1))
                    at_grp_ev[(i, grp)] = dve.n
                    for kc in range(k0, k1):
                        for dh in range(2):
                            bank2 = pyb[(i % 2) * 2 + dh]
                            w2 = [(dve, at_grp_ev[(i, grp)])]
                            if kc == 0 and (i - 2, dh) in yev_mark:
                                w2.append((dve, yev_mark[(i - 2, dh)]))
                            p.op('tensor', (lambda e, kc=kc, dh=dh, bank2=bank2,
                                            i=i: e.matmul(
                                                bank2[:, :], atile[:, kc % 8, :],
                                                vtm[:, kc, dh * 512:(dh + 1) * 512],
                                                start=(kc == 0),
                                                stop=(kc == i))),
                                 waits=tuple(w2) if (kc == k0 or kc == 0) else (),
                                 inc=(pe, 1) if kc == i else None)
                at_evall[i] = at_grp_ev[(i, ngrp - 1)]
                av_done[i] = pe.n

            at_evall = {}

            def emit_out(i):
                buf = i % 2
                cb = cols[:, (i % 2) * 16 + 8:(i % 2) * 16 + 16]
                for dh in range(2):
                    bank = pyb[(i % 2) * 2 + dh]
                    p.op('vector', (lambda e, i=i, buf=buf, dh=dh, bank=bank:
                                    e.tensor_scalar(
                                        ytile[:, buf, dh * 512:(dh + 1) * 512],
                                        bank[:, :], rcol[:, i % 2:i % 2 + 1],
                                        None, OP.mult)),
                         waits=((pe, av_done[i]), (dve, soft_mark[i][1])),
                         inc=(dve, 1))
                    yev_mark[(i, dh)] = dve.n
                w = ((st, out_mark[i - 2]),) if i - 2 in out_mark else ()
                p.op('sync', (lambda e, i=i, buf=buf: e.dma_start(
                    xstg[:, buf, :], xs[:, i, :])), waits=w, inc=(ld, 16))
                ldm = ld.n
                yready = ((dve, yev_mark[(i, 1)]),)
                ln_stats(p, g, ytile[:, buf, :], sqd[:, :], cb, g.epsc[:, 0:1],
                         waits_v=yready, waits_s=yready)
                wv = [(ld, ldm), (dve, dve.n)]
                if i - 2 in out_mark:
                    wv.append((st, out_mark[i - 2]))
                p.op('vector', (lambda e, buf=buf, cb=cb: e.tensor_scalar(
                    xstg[:, buf, :], xstg[:, buf, :], cb[:, 1:2], None,
                    OP.add)), waits=tuple(wv))
                p.op('vector', (lambda e, buf=buf, cb=cb: e.scalar_tensor_tensor(
                    xstg[:, buf, :], ytile[:, buf, :], cb[:, 0:1],
                    xstg[:, buf, :], OP.mult, OP.add)), inc=(dve, 1))
                vm = dve.n
                p.op('sync', (lambda e, i=i, buf=buf: e.dma_start(
                    xd[:, i, :], xstg[:, buf, :])), waits=((dve, vm),),
                    inc=(st, 16))
                out_mark[i] = st.n

            for i in range(NT + 1):
                if i < NT:
                    emit_scores(i)
                if i >= 1:
                    emit_av(i - 1)
                    emit_out(i - 1)
            need = st.n
            p.op('sync', lambda e: e.nop(), waits=((st, need),))
            p.emit(nc)



def phase_l1f(g, x_src, x_dst, dbg=None):
    nc = g.nc
    dt = g.dt
    c = g.c
    ld, cast, act, dve, pe, st = c.ld, c.cast, c.act, c.dve, c.pe, c.st
    xs = x_src.rearrange("(t p) d -> p t d", p=128)
    xd = x_dst.rearrange("(t p) d -> p t d", p=128)
    xn4 = g.scr["sxn4"]
    xn4v = xn4.rearrange("(t p) d -> p t d", p=128)
    MSC = 4
    MC = M1 // (128 * MSC)

    with contextlib.ExitStack() as ph:
        sqd = ph.enter_context(nc.sbuf_tensor("l1f_sqd", [128, D], F32))
        cols = ph.enter_context(nc.sbuf_tensor("l1f_cols", [128, 32], F32))
        mw2 = ph.enter_context(nc.sbuf_tensor("l1f_mw", [128, NT * 8], F32))
        qkv_st = ph.enter_context(contextlib.ExitStack())
        qT = qkv_st.enter_context(nc.sbuf_tensor("l1f_qT", [128, ND, S], BF16))
        kT2 = qkv_st.enter_context(nc.sbuf_tensor("l1f_kT2", [128, ND, S], BF16))
        vtm = qkv_st.enter_context(nc.sbuf_tensor("l1f_v", [128, NT, 512], BF16))
        hT_st = ph.enter_context(contextlib.ExitStack())
        hT = hT_st.enter_context(nc.sbuf_tensor("l1f_hT", [128, ND, S], BF16))

        with contextlib.ExitStack() as b0:
            xstg = b0.enter_context(nc.sbuf_tensor("l1f_xstg", [128, 2, D], F32))
            nrm = b0.enter_context(nc.sbuf_tensor("l1f_nrm", [128, 2, D], F32))
            pt = [b0.enter_context(nc.psum_tensor(f"l1f_pt{i}", [128, 512], F32))
                  for i in range(2)]

            def tslot(dc):
                return pt[dc // 4][:, (dc % 4) * 128:(dc % 4) * 128 + 128]

            p = Plan(g)
            tp_mark = {}
            ev_mark = {}
            for t in range(NT):
                buf = t % 2
                cb1 = cols[:, buf * 16:buf * 16 + 8]
                cb2 = cols[:, buf * 16 + 8:buf * 16 + 16]
                w = ((act, ev_mark[t - 2]),) if t - 2 in ev_mark else ()
                p.op('sync', (lambda e, t=t, buf=buf: e.dma_start(
                    xstg[:, buf, :], xs[:, t, :])), waits=w, inc=(ld, 16))
                ldm = ld.n
                ln_stats(p, g, xstg[:, buf, :], sqd[:, :], cb1, g.epsc[:, 0:1],
                         waits_v=((ld, ldm),), waits_s=((ld, ldm),))
                w = [(dve, dve.n)]
                if t - 2 in tp_mark:
                    w.append((pe, tp_mark[t - 2]))
                p.op('scalar', (lambda e, buf=buf, cb1=cb1: e.activation(
                    nrm[:, buf, :], xstg[:, buf, :], AF.Identity,
                    bias=cb1[:, 1:2], scale=cb1[:, 0:1])), waits=tuple(w))
                xn_mark = act.n
                p.op('sync', (lambda e, t=t, buf=buf: e.dma_start(
                    xn4v[:, t, :], nrm[:, buf, :])),
                    waits=((act, xn_mark),), inc=(st, 16))
                p.op('scalar', (lambda e, buf=buf, cb2=cb2: e.activation(
                    sqd[:, :], nrm[:, buf, :], AF.Square,
                    accum_out=cb2[:, 2:3])))
                mka = act.n
                p.op('vector', (lambda e, cb2=cb2: e.tensor_scalar(
                    cb2[:, 3:4], cb2[:, 2:3], 1.0 / D, None, OP.mult)),
                    waits=((act, mka),))
                p.op('scalar', (lambda e, cb2=cb2: e.activation(
                    cb2[:, 4:5], cb2[:, 3:4], AF.Sqrt, bias=g.epsc[:, 2:3])),
                    waits=((dve, dve.n),))
                p.op('vector', (lambda e, cb2=cb2: e.reciprocal(
                    cb2[:, 0:1], cb2[:, 4:5])), waits=((act, act.n),),
                    inc=(dve, 1))
                p.op('vector', (lambda e, buf=buf, cb2=cb2: e.tensor_scalar(
                    nrm[:, buf, :], nrm[:, buf, :], cb2[:, 0:1], None,
                    OP.mult)), inc=(dve, 1))
                h_mark = dve.n
                tpg = []
                for dc in range(ND):
                    w = [(dve, h_mark)]
                    if t - 1 in ev_mark:
                        w.append((act, ev_mark[t - 1]))
                    p.op('tensor', (lambda e, buf=buf, dc=dc:
                                    e.transpose(tslot(dc),
                                                nrm[:, buf, dc * 128:dc * 128 + 128],
                                                g.ident32[:, :])),
                         waits=tuple(w), inc=(pe, 1))
                    tpg.append(pe.n)
                tp_mark[t] = pe.n
                for dc in range(ND):
                    p.op('scalar', (lambda e, t=t, dc=dc: e.activation(
                        hT[:, dc, t * 128:t * 128 + 128], tslot(dc),
                        AF.Identity, scale=g.lnc[:, dc:dc + 1])),
                        waits=((pe, tpg[3 if dc < 4 else 7]),))
                ev_mark[t] = act.n
            p.emit(nc)

        with contextlib.ExitStack() as b1:
            sb1 = lambda n, s, d=F32: b1.enter_context(nc.sbuf_tensor(n, s, d))
            wq16 = sb1("l1f_wq16", [128, ND, D], BF16)
            wk16 = sb1("l1f_wk16", [128, ND, 512], BF16)
            wv16 = sb1("l1f_wv16", [128, ND, 512], BF16)
            kTc = sb1("l1f_kTc", [128, 4, S], BF16)
            tq = [sb1(f"l1f_ropetab{i}", [128, S], BF16) for i in range(4)]
            t1 = sb1("l1f_t1", [128, S], BF16)
            qsw = sb1("l1f_qsw", [128, S], BF16)
            pp = [b1.enter_context(nc.psum_tensor(f"l1f_pp{i}", [128, 512], F32))
                  for i in range(4)]
            p = Plan(g)
            for i, nm_ in enumerate(("ropeq_cos", "ropeq_sin",
                                     "ropek_cos", "ropek_sin")):
                p.op('gpsimd', (lambda e, i=i, nm_=nm_: e.dma_start(
                    tq[i][:, :], dt[nm_][:, :])), inc=(cast, 16))
            p.op('gpsimd', lambda e: e.dma_start(
                wq16[:, :, :],
                dt["wq1"].rearrange("(c p) o -> p c o", p=128)), inc=(cast, 16))
            p.op('gpsimd', lambda e: e.dma_start(
                wk16[:, :, :],
                dt["wk1"].rearrange("(c p) o -> p c o", p=128)), inc=(cast, 16))
            p.op('gpsimd', lambda e: e.dma_start(
                wv16[:, :, :],
                dt["wv1"].rearrange("(c p) o -> p c o", p=128)), inc=(cast, 16))
            w_mark = cast.n

            ev_d = {}
            for oc in range(ND):
                for sg in range(4):
                    gidx = oc * 4 + sg
                    bank = pp[gidx % 2]
                    w = [(cast, w_mark)] if gidx == 0 else []
                    if gidx - 2 in ev_d:
                        w.append((act, ev_d[gidx - 2]))
                    for hc in range(ND):
                        p.op('tensor', (lambda e, oc=oc, sg=sg, hc=hc,
                                        bank=bank: e.matmul(
                                            bank[:, :],
                                            wq16[:, hc, oc * 128:oc * 128 + 128],
                                            hT[:, hc, sg * 512:(sg + 1) * 512],
                                            start=(hc == 0),
                                            stop=(hc == ND - 1))),
                             waits=tuple(w) if hc == 0 else (),
                             inc=(pe, 1) if hc == ND - 1 else None)
                    mmm = pe.n
                    p.op('scalar', (lambda e, oc=oc, sg=sg, bank=bank:
                                    e.activation(
                                        qT[:, oc, sg * 512:(sg + 1) * 512],
                                        bank[:, :], AF.Identity)),
                         waits=((pe, mmm),))
                    ev_d[gidx] = act.n
            for ock in range(4):
                for sg in range(4):
                    gidx = 32 + ock * 4 + sg
                    bank = pp[gidx % 2]
                    w = []
                    if gidx - 2 in ev_d:
                        w.append((act, ev_d[gidx - 2]))
                    for hc in range(ND):
                        p.op('tensor', (lambda e, ock=ock, sg=sg, hc=hc,
                                        bank=bank: e.matmul(
                                            bank[:, :],
                                            wk16[:, hc, ock * 128:ock * 128 + 128],
                                            hT[:, hc, sg * 512:(sg + 1) * 512],
                                            start=(hc == 0),
                                            stop=(hc == ND - 1))),
                             waits=tuple(w) if hc == 0 else (),
                             inc=(pe, 1) if hc == ND - 1 else None)
                    mmm = pe.n
                    p.op('scalar', (lambda e, ock=ock, sg=sg, bank=bank:
                                    e.activation(
                                        kTc[:, ock, sg * 512:(sg + 1) * 512],
                                        bank[:, :], AF.Identity)),
                         waits=((pe, mmm),))
                    ev_d[gidx] = act.n
            for ti in range(NT):
                gidx = 48 + ti
                bank = pp[2 + ti % 2]
                w = []
                if gidx - 2 in ev_d:
                    w.append((act, ev_d[gidx - 2]))
                for hc in range(ND):
                    p.op('tensor', (lambda e, ti=ti, hc=hc, bank=bank:
                                    e.matmul(
                                        bank[:, :],
                                        hT[:, hc, ti * 128:ti * 128 + 128],
                                        wv16[:, hc, :],
                                        start=(hc == 0), stop=(hc == ND - 1))),
                         waits=tuple(w) if hc == 0 else (),
                         inc=(pe, 1) if hc == ND - 1 else None)
                mmm = pe.n
                p.op('scalar', (lambda e, ti=ti, bank=bank: e.activation(
                    vtm[:, ti, :], bank[:, :], AF.Identity)),
                     waits=((pe, mmm),))
                ev_d[gidx] = act.n
            qkv_act = act.n

            rope_dma = {}
            for ci in range(12):
                isq = ci < ND
                srcT = qT if isq else kTc
                cc = ci if isq else ci - ND
                ctab = tq[0] if isq else tq[2]
                stab = tq[1] if isq else tq[3]
                w = [(act, qkv_act), (dve, dve.n)]
                for j, (d0, s0) in enumerate(((0, 32), (32, 0),
                                              (64, 96), (96, 64))):
                    p.op('sync', (lambda e, srcT=srcT, cc=cc, d0=d0, s0=s0:
                                  e.dma_start(qsw[d0:d0 + 32, :],
                                              srcT[s0:s0 + 32, cc, :])),
                         waits=tuple(w) if j == 0 else (), inc=(ld, 16))
                dm = ld.n
                p.op('vector', (lambda e, srcT=srcT, cc=cc, ctab=ctab:
                                e.tensor_tensor(t1[:, :], srcT[:, cc, :],
                                                ctab[:, :], OP.mult)),
                     waits=((cast, w_mark), (act, qkv_act)))
                p.op('vector', (lambda e, stab=stab: e.tensor_tensor(
                    qsw[:, :], qsw[:, :], stab[:, :], OP.mult)),
                     waits=((ld, dm),))
                p.op('vector', (lambda e, srcT=srcT, cc=cc: e.tensor_tensor(
                    srcT[:, cc, :], t1[:, :], qsw[:, :], OP.add)),
                     inc=(dve, 1))
                rope_dma[ci] = dve.n
            for gkv in range(ND):
                sb_ = (gkv % 2) * 64
                for half in range(2):
                    p.op('sync', (lambda e, gkv=gkv, half=half, sb_=sb_:
                                  e.dma_start(
                                      kT2[half * 64:half * 64 + 64, gkv, :],
                                      kTc[sb_:sb_ + 64, gkv // 2, :])),
                         waits=((dve, rope_dma[11]),), inc=(ld, 16))
            kdup_mark = ld.n
            p.op('sync', lambda e: e.nop(), waits=((ld, kdup_mark),))
            p.emit(nc)
        hT_st.close()
        u2 = ph.enter_context(nc.sbuf_tensor("l1f_u2", [128, NT, D], F32,
                                             side="right"))

        with contextlib.ExitStack() as b2:
            sb2 = lambda n, s, d=F32: b2.enter_context(nc.sbuf_tensor(n, s, d))
            srow = sb2("l1f_srow", [128, 1, S])
            arow = sb2("l1f_arow", [128, 2, S], BF16)
            atile = sb2("l1f_atile", [128, 8, 128], BF16)
            aoT = sb2("l1f_aoT", [128, ND, 128], BF16)
            wo16 = sb2("l1f_wo16", [128, ND, D], BF16)
            xstg = sb2("l1f_xstg2", [128, 2, D])
            rc2 = sb2("l1f_rc2", [128, 4])
            psc = [b2.enter_context(nc.psum_tensor(f"l1f_psc{i}", [128, 512], F32))
                   for i in range(2)]
            pta = [b2.enter_context(nc.psum_tensor(f"l1f_pta{i}", [128, 1024],
                                                   BF16)) for i in range(2)]
            pav = [b2.enter_context(nc.psum_tensor(f"l1f_pav{i}", [128, 512], F32))
                   for i in range(2)]
            pwo = [b2.enter_context(nc.psum_tensor(f"l1f_pwo{i}", [128, 512], F32))
                   for i in range(2)]

            p = Plan(g)
            p.op('gpsimd', lambda e: e.dma_start(
                wo16[:, :, :],
                dt["wo1"].rearrange("(c p) d -> p c d", p=128)), inc=(cast, 16))
            wo_mark = cast.n

            bank_last_sc = {}
            soft_mark = {}
            tp_last = {}
            at_evall = {}
            ao_ev = {}
            u2_mark = {}

            def emit_scores(i, h):
                buf = h % 2
                gkv = h // 2
                base = (h % 2) * 64
                nkb = i // 4 + 1
                prev = (i, h - 1) if h >= 1 else ((i - 1, 15) if i >= 1 else None)
                for kb in range(nkb):
                    N = 512 if kb < i // 4 else (i % 4 + 1) * 128
                    bank = psc[kb % 2]
                    w = []
                    if (kb % 2) in bank_last_sc:
                        w.append((act, bank_last_sc[kb % 2][1]))
                    p.op('tensor', (lambda e, i=i, h=h, kb=kb, N=N, bank=bank,
                                    base=base, gkv=gkv: e.matmul(
                                        bank[:, 0:N],
                                        qT[base:base + 64, h // 2,
                                           i * 128:i * 128 + 128],
                                        kT2[base:base + 64, gkv,
                                            kb * 512:kb * 512 + N],
                                        start=True, stop=True)),
                         waits=tuple(w), inc=(pe, 1))
                    mmm = pe.n
                    wv = [(pe, mmm)]
                    if kb == 0 and prev in soft_mark:
                        wv.append((act, soft_mark[prev][0]))
                    p.op('scalar', (lambda e, kb=kb, bank=bank, N=N:
                                    e.activation(
                                        srow[:, 0, kb * 512:kb * 512 + N],
                                        bank[:, 0:N], AF.Identity)),
                         waits=tuple(wv))
                    bank_last_sc[kb % 2] = ('act', act.n)
                W_ = (i + 1) * 128
                cb = cols[:, buf * 16:buf * 16 + 8]
                p.op('vector', (lambda e, i=i: e.tensor_tensor(
                    srow[:, 0, i * 128:i * 128 + 128],
                    srow[:, 0, i * 128:i * 128 + 128],
                    g.causal32[:, :], OP.add)),
                    waits=((act, act.n),))
                p.op('vector', (lambda e, W_=W_, cb=cb: e.reduce_max(
                    cb[:, 0:1], srow[:, 0, 0:W_], axis=AX, negate=True)))
                w = [(dve, dve.n)]
                key = (i, h - 2) if (i, h - 2) in tp_last else (
                    (i - 1, 14 + h) if h < 2 else None)
                if key in tp_last:
                    w.append((pe, tp_last[key]))
                p.op('scalar', (lambda e, buf=buf, W_=W_, cb=cb: e.activation(
                    arow[:, buf, 0:W_], srow[:, 0, 0:W_], AF.Exp,
                    bias=cb[:, 0:1], accum_out=cb[:, 1:2])), waits=tuple(w))
                em = act.n
                p.op('vector', (lambda e, buf=buf, cb=cb: e.reciprocal(
                    rc2[:, buf:buf + 1], cb[:, 1:2])),
                    waits=((act, em),))
                p.op('vector', (lambda e, buf=buf, W_=W_: e.tensor_scalar(
                    arow[:, buf, 0:W_], arow[:, buf, 0:W_],
                    rc2[:, buf:buf + 1], None, OP.mult)), inc=(dve, 1))
                soft_mark[(i, h)] = (em, dve.n)

            def emit_av(i, h):
                buf = h % 2
                gkv = h // 2
                base = (h % 2) * 64
                bank_av = pav[h % 2]
                ngrp = (i + 1 + 7) // 8
                for grp in range(ngrp):
                    k0 = grp * 8
                    k1 = min(k0 + 8, i + 1)
                    bank = pta[grp % 2]
                    w = [(act, soft_mark[(i, h)][0]),
                         (dve, soft_mark[(i, h)][1])]
                    if (i, h, grp - 2) in at_evall:
                        w.append((dve, at_evall[(i, h, grp - 2)]))
                    else:
                        pk = (i, h - 1) if h >= 1 else (i - 1, 15)
                        if pk in at_evall:
                            w.append((dve, at_evall[pk]))
                    for kc in range(k0, k1):
                        p.op('tensor', (lambda e, buf=buf, kc=kc, bank=bank,
                                        k0=k0: e.transpose(
                                            bank[:, (kc - k0) * 128:(kc - k0) * 128 + 128],
                                            arow[:, buf, kc * 128:kc * 128 + 128],
                                            g.ident16[:, :])),
                             waits=tuple(w) if kc == k0 else (), inc=(pe, 1))
                    gm = pe.n
                    if grp == ngrp - 1:
                        tp_last[(i, h)] = gm
                    for kc in range(k0, k1):
                        p.op('vector', (lambda e, kc=kc, bank=bank, k0=k0:
                                        e.tensor_copy(
                                            atile[:, kc % 8, :],
                                            bank[:, (kc - k0) * 128:(kc - k0) * 128 + 128])),
                             waits=((pe, gm),), inc=(dve, 1))
                    at_evall[(i, h, grp)] = dve.n
                    for kc in range(k0, k1):
                        w2 = [(dve, at_evall[(i, h, grp)])]
                        if kc == 0:
                            pk = (i, h - 2) if h >= 2 else (i - 1, 14 + h)
                            if pk in ao_ev:
                                w2.append((dve, ao_ev[pk]))
                        tp_ = (0, base)
                        p.op('tensor', (lambda e, kc=kc, i=i, gkv=gkv,
                                        bank_av=bank_av, base=base, tp_=tp_:
                                        e.matmul(
                                            bank_av[base:base + 64, 0:128],
                                            vtm[:, kc, gkv * 64:gkv * 64 + 64],
                                            atile[:, kc % 8, :],
                                            start=(kc == 0), stop=(kc == i),
                                            tile_position=tp_)),
                             waits=tuple(w2) if kc in (k0, 0) else (),
                             inc=(pe, 1) if kc == i else None)
                at_evall[(i, h)] = at_evall[(i, h, ngrp - 1)]
                avm = pe.n
                p.op('vector', (lambda e, h=h, base=base, bank_av=bank_av:
                                e.tensor_copy(
                                    aoT[base:base + 64, h // 2, :],
                                    bank_av[base:base + 64, 0:128])),
                     waits=((pe, avm),), inc=(dve, 1))
                ao_ev[(i, h)] = dve.n

            def emit_wo(i):
                buf = i % 2
                w = ((dve, u2_mark[i - 2][0]),) if i - 2 in u2_mark else ()
                p.op('sync', (lambda e, i=i, buf=buf: e.dma_start(
                    xstg[:, buf, :], xn4v[:, i, :])), waits=w, inc=(ld, 16))
                ldm = ld.n
                for dh in range(2):
                    bank = pwo[dh]
                    w = [(cast, wo_mark)]
                    for hh in (14, 15):
                        w.append((dve, ao_ev[(i, hh)]))
                    if i - 1 in u2_mark:
                        w.append((dve, u2_mark[i - 1][0]))
                    for oc in range(ND):
                        p.op('tensor', (lambda e, oc=oc, dh=dh, bank=bank:
                                        e.matmul(
                                            bank[:, :], aoT[:, oc, :],
                                            wo16[:, oc, dh * 512:(dh + 1) * 512],
                                            start=(oc == 0),
                                            stop=(oc == ND - 1))),
                             waits=tuple(w) if oc == 0 else (),
                             inc=(pe, 1) if oc == ND - 1 else None)
                    mmm = pe.n
                    p.op('vector', (lambda e, i=i, buf=buf, dh=dh, bank=bank:
                                    e.tensor_add(
                                        u2[:, i, dh * 512:(dh + 1) * 512],
                                        xstg[:, buf, dh * 512:(dh + 1) * 512],
                                        bank[:, :])),
                         waits=((pe, mmm), (ld, ldm)), inc=(dve, 1))
                u2_mark[i] = (dve.n, st.n)

            for i in range(NT):
                for h in range(NH + 1):
                    if h < NH:
                        emit_scores(i, h)
                    if h >= 1:
                        emit_av(i, h - 1)
                emit_wo(i)
            p.emit(nc)
        qkv_st.close()

        if dbg == 21:
            pd = Plan(g)
            for t in range(NT):
                pd.op('sync', (lambda e, t=t: e.dma_start(xd[:, t, :],
                                                          u2[:, t, :])),
                      inc=(st, 16))
            need = st.n
            pd.op('sync', lambda e: e.nop(), waits=((st, need),))
            pd.emit(nc)
            return

        h2T_st = ph.enter_context(contextlib.ExitStack())
        h2T = h2T_st.enter_context(nc.sbuf_tensor("l1f_h2T", [128, ND, S], BF16))
        with contextlib.ExitStack() as b3:
            nrm = b3.enter_context(nc.sbuf_tensor("l1f_nrm3", [128, 2, D], F32))
            xf32 = b3.enter_context(nc.sbuf_tensor("l1f_xf32", [128, D], F32))
            rwc = b3.enter_context(nc.sbuf_tensor("l1f_rwc", [128, ND, E1], F32))
            lg = b3.enter_context(nc.sbuf_tensor("l1f_lg", [128, 2 * 8], F32))
            prb = b3.enter_context(nc.sbuf_tensor("l1f_prb", [128, 2 * 8], F32))
            pt = [b3.enter_context(nc.psum_tensor(f"l1f_pt3{i}", [128, 512], F32))
                  for i in range(2)]
            prt = b3.enter_context(nc.psum_tensor("l1f_prt", [128, 16], F32))

            def tslot(dc):
                return pt[dc // 4][:, (dc % 4) * 128:(dc % 4) * 128 + 128]

            p = Plan(g)
            p.op('sync', lambda e: e.dma_start(
                rwc[:, :, :], dt["rw1_pc"][:, :, :]), inc=(ld, 16))
            rw_mark = ld.n
            ev_mark = {}
            rt_mark = None
            lg_mark = {}
            for t in range(NT):
                buf = t % 2
                cb2 = cols[:, buf * 16 + 8:buf * 16 + 16]
                p.op('scalar', (lambda e, t=t: e.activation(
                    sqd[:, :], u2[:, t, :], AF.Square,
                    accum_out=cb2[:, 2:3])))
                mka = act.n
                p.op('vector', (lambda e, cb2=cb2: e.tensor_scalar(
                    cb2[:, 3:4], cb2[:, 2:3], 1.0 / D, None, OP.mult)),
                    waits=((act, mka),))
                p.op('scalar', (lambda e, cb2=cb2: e.activation(
                    cb2[:, 4:5], cb2[:, 3:4], AF.Sqrt, bias=g.epsc[:, 2:3])),
                    waits=((dve, dve.n),))
                p.op('vector', (lambda e, cb2=cb2: e.reciprocal(
                    cb2[:, 0:1], cb2[:, 4:5])), waits=((act, act.n),),
                    inc=(dve, 1))
                w = [(dve, dve.n)]
                if t - 2 in ev_mark:
                    w.append((pe, ev_mark[t - 2][0]))
                p.op('vector', (lambda e, t=t, buf=buf, cb2=cb2:
                                e.tensor_scalar(
                    nrm[:, buf, :], u2[:, t, :], cb2[:, 0:1], None, OP.mult)),
                    waits=tuple(w), inc=(dve, 1))
                h_mark = dve.n
                tpg = []
                for dc in range(ND):
                    w = [(dve, h_mark)]
                    if t - 1 in ev_mark:
                        w.append((act, ev_mark[t - 1][1]))
                    p.op('tensor', (lambda e, buf=buf, dc=dc:
                                    e.transpose(tslot(dc),
                                                nrm[:, buf, dc * 128:dc * 128 + 128],
                                                g.ident32[:, :])),
                         waits=tuple(w), inc=(pe, 1))
                    tpg.append(pe.n)
                for dc in range(ND):
                    p.op('scalar', (lambda e, t=t, dc=dc: e.activation(
                        h2T[:, dc, t * 128:t * 128 + 128], tslot(dc),
                        AF.Identity, scale=g.lnc[:, 8 + dc:8 + dc + 1])),
                        waits=((pe, tpg[3 if dc < 4 else 7]),))
                for dc in range(ND):
                    w = [(pe, tpg[3 if dc < 4 else 7])]
                    if dc == 0 and rt_mark is not None:
                        w.append((pe, rt_mark))
                    p.op('scalar', (lambda e, dc=dc: e.activation(
                        xf32[:, dc * 128:dc * 128 + 128], tslot(dc),
                        AF.Identity, scale=g.lnc[:, 8 + dc:8 + dc + 1])),
                        waits=tuple(w))
                ev2 = act.n
                ev_mark[t] = (pe.n, ev2)
                w = [(act, ev2), (ld, rw_mark)]
                if t - 1 in lg_mark:
                    w.append((dve, lg_mark[t - 1]))
                for dc in range(ND):
                    p.op('tensor', (lambda e, dc=dc: e.matmul(
                        prt[:, 0:E1], xf32[:, dc * 128:dc * 128 + 128],
                        rwc[:, dc, :],
                        start=(dc == 0), stop=(dc == ND - 1))),
                        waits=tuple(w) if dc == 0 else (),
                        inc=(pe, 1) if dc == ND - 1 else None)
                rt_mark = pe.n
                L = lg[:, buf * 8:buf * 8 + 8]
                P_ = prb[:, buf * 8:buf * 8 + 8]
                c5, c6, c7 = cb2[:, 5:6], cb2[:, 6:7], cb2[:, 7:8]
                p.op('vector', (lambda e, L=L: e.tensor_copy(L, prt[:, 0:E1])),
                     waits=((pe, rt_mark),), inc=(dve, 1))
                lg_mark[t] = dve.n
                p.op('vector', (lambda e, L=L, c5=c5: e.reduce_max(
                    c5, L, axis=AX, negate=True)))
                p.op('scalar', (lambda e, L=L, P_=P_, c5=c5, c6=c6:
                                e.activation(P_, L, AF.Exp, bias=c5,
                                             accum_out=c6)),
                    waits=((dve, dve.n),))
                p.op('vector', (lambda e, c6=c6: e.reciprocal(c6, c6)),
                     waits=((act, act.n),))
                p.op('vector', (lambda e, P_=P_, c6=c6: e.tensor_scalar(
                    P_, P_, c6, None, OP.mult)))
                p.op('vector', (lambda e, P_=P_, c5=c5: e.reduce_max(
                    c5, P_, axis=AX, negate=True)))
                p.op('vector', (lambda e, L=L, P_=P_, c5=c5: e.tensor_scalar(
                    L, P_, c5, 0.0, OP.add, OP.is_ge)))
                p.op('vector', (lambda e, L=L, P_=P_: e.scalar_tensor_tensor(
                    L, L, NEG, P_, OP.mult, OP.add)))
                p.op('vector', (lambda e, L=L, c7=c7: e.reduce_max(
                    c7, L, axis=AX, negate=True)))
                p.op('vector', (lambda e, L=L, P_=P_, c7=c7: e.tensor_scalar(
                    L, P_, c7, 0.0, OP.add, OP.is_ge)))
                p.op('vector', (lambda e, L=L, P_=P_: e.tensor_mul(P_, P_, L)))
                p.op('vector', (lambda e, P_=P_, c7=c7: e.reduce_sum(
                    c7, P_, axis=AX)))
                p.op('vector', (lambda e, c7=c7: e.reciprocal(c7, c7)))
                p.op('vector', (lambda e, t=t, P_=P_, c7=c7: e.tensor_scalar(
                    mw2[:, t * 8:t * 8 + 8], P_, c7, None, OP.mult)),
                    inc=(dve, 1))
                lg_mark[t] = dve.n
            p.emit(nc)

        with contextlib.ExitStack() as b4:
            sb4 = lambda n, s, d=F32: b4.enter_context(nc.sbuf_tensor(n, s, d))
            wg16 = sb4("l1f_wg16", [128, 2, ND, 128 * MC], BF16)
            wu16 = sb4("l1f_wu16", [128, 2, ND, 128 * MC], BF16)
            wd16 = sb4("l1f_wd16", [128, 2, MC, D], BF16)
            actb = sb4("l1f_act", [128, MC, 512], BF16)
            stmp = sb4("l1f_stmp", [128, 2, 512])
            pg = [b4.enter_context(nc.psum_tensor(f"l1f_pg{i}", [128, 512], F32))
                  for i in range(2)]
            pu = [b4.enter_context(nc.psum_tensor(f"l1f_pu{i}", [128, 512], F32))
                  for i in range(2)]
            pyy = [b4.enter_context(nc.psum_tensor(f"l1f_py{i}", [128, 512], F32))
                   for i in range(4)]
            p = Plan(g)

            wld = {}
            gu_mark = {}
            gu_lastmm = {}
            dn_tg_done = {}
            dn_done = {}
            uev = {}
            NEI = E1 * MSC
            for ei in range(NEI):
                e_ = ei // MSC
                ms = ei % MSC
                m0 = ms * MC * 128
                wb = ei % 2
                w = []
                if ei - 2 in gu_lastmm:
                    w.append((pe, gu_lastmm[ei - 2]))
                p.op('gpsimd', (lambda e, e_=e_, m0=m0, wb=wb: e.dma_start(
                    wg16[:, wb, :, :],
                    dt["wg1"][e_].rearrange("(c p) m -> p c m", p=128)
                    [:, :, m0:m0 + MC * 128])), waits=tuple(w),
                    inc=(cast, 16))
                p.op('gpsimd', (lambda e, e_=e_, m0=m0, wb=wb: e.dma_start(
                    wu16[:, wb, :, :],
                    dt["wu1"][e_].rearrange("(c p) m -> p c m", p=128)
                    [:, :, m0:m0 + MC * 128])), inc=(cast, 16))
                w = []
                if ei - 2 in dn_done:
                    w.append((pe, dn_done[ei - 2]))
                p.op('gpsimd', (lambda e, e_=e_, m0=m0, wb=wb: e.dma_start(
                    wd16[:, wb, :, :],
                    dt["wd1"][e_].rearrange("(c p) d -> p c d", p=128)
                    [:, m0 // 128:m0 // 128 + MC, :])), waits=tuple(w),
                    inc=(cast, 16))
                wld[ei] = cast.n
                if True:

                    for tg in range(4):
                        for mc in range(MC):
                            gb = mc % 2
                            w = []
                            if mc == 0 and tg == 0:
                                w.append((cast, wld[ei]))
                            key = None
                            if (ei, tg, mc - 2) in gu_mark:
                                key = (ei, tg, mc - 2)
                            elif tg > 0 and (ei, tg - 1, MC - 2 + mc) in gu_mark:
                                key = (ei, tg - 1, MC - 2 + mc)
                            elif ei > 0 and (ei - 1, 3, MC - 2 + mc) in gu_mark:
                                key = (ei - 1, 3, MC - 2 + mc)
                            if key is not None:
                                w.append((dve, gu_mark[key]))
                            if mc == 0 and (ei, tg - 1) in dn_tg_done:
                                w.append((pe, dn_tg_done[(ei, tg - 1)]))
                            elif mc == 0 and tg == 0 and (ei - 1, 3) in dn_tg_done:
                                w.append((pe, dn_tg_done[(ei - 1, 3)]))
                            for dc in range(ND):
                                p.op('tensor', (lambda e, mc=mc, dc=dc, tg=tg,
                                                wb=wb, gb=gb: e.matmul(
                                                    pg[gb][:, :],
                                                    wg16[:, wb, dc, mc * 128:mc * 128 + 128],
                                                    h2T[:, dc, tg * 512:(tg + 1) * 512],
                                                    start=(dc == 0),
                                                    stop=(dc == ND - 1))),
                                     waits=tuple(w) if dc == 0 else (),
                                     inc=(pe, 1) if dc == ND - 1 else None)
                            for dc in range(ND):
                                p.op('tensor', (lambda e, mc=mc, dc=dc, tg=tg,
                                                wb=wb, gb=gb: e.matmul(
                                                    pu[gb][:, :],
                                                    wu16[:, wb, dc, mc * 128:mc * 128 + 128],
                                                    h2T[:, dc, tg * 512:(tg + 1) * 512],
                                                    start=(dc == 0),
                                                    stop=(dc == ND - 1))),
                                     inc=(pe, 1) if dc == ND - 1 else None)
                            mmg = pe.n
                            p.op('scalar', (lambda e, gb=gb: e.activation(
                                stmp[:, gb, :], pg[gb][:, :], AF.Silu)),
                                waits=((pe, mmg),))
                            sm = act.n
                            p.op('vector', (lambda e, mc=mc, gb=gb:
                                            e.tensor_tensor(
                                actb[:, mc, :], stmp[:, gb, :], pu[gb][:, :],
                                OP.mult)), waits=((act, sm), (pe, mmg)),
                                inc=(dve, 1))
                            gu_mark[(ei, tg, mc)] = dve.n
                        if tg == 3:
                            gu_lastmm[ei] = pe.n
                        for tt in range(4):
                            t_glob = tg * 4 + tt
                            for dh in range(2):
                                bank = pyy[(tt % 2) * 2 + dh]
                                w = [(dve, gu_mark[(ei, tg, MC - 1)])]
                                key = (ei, tg, tt - 2, dh)
                                if key in uev:
                                    w.append((dve, uev[key]))
                                elif tg > 0 and (ei, tg - 1, 2 + tt, dh) in uev:
                                    w.append((dve, uev[(ei, tg - 1, 2 + tt, dh)]))
                                elif ei > 0 and (ei - 1, 3, 2 + tt, dh) in uev:
                                    w.append((dve, uev[(ei - 1, 3, 2 + tt, dh)]))
                                for mc in range(MC):
                                    p.op('tensor', (lambda e, mc=mc, tt=tt,
                                                    dh=dh, bank=bank, wb=wb:
                                                    e.matmul(
                                                        bank[:, :],
                                                        actb[:, mc, tt * 128:tt * 128 + 128],
                                                        wd16[:, wb, mc, dh * 512:(dh + 1) * 512],
                                                        start=(mc == 0),
                                                        stop=(mc == MC - 1))),
                                         waits=tuple(w) if mc == 0 else (),
                                         inc=(pe, 1) if mc == MC - 1 else None)
                                mmm = pe.n
                                p.op('vector', (lambda e, e_=e_, t_glob=t_glob,
                                                dh=dh, bank=bank:
                                                e.scalar_tensor_tensor(
                                                    u2[:, t_glob, dh * 512:(dh + 1) * 512],
                                                    bank[:, :],
                                                    mw2[:, t_glob * 8 + e_:t_glob * 8 + e_ + 1],
                                                    u2[:, t_glob, dh * 512:(dh + 1) * 512],
                                                    OP.mult, OP.add)),
                                     waits=((pe, mmm),), inc=(dve, 1))
                                uev[(ei, tg, tt, dh)] = dve.n
                        dn_tg_done[(ei, tg)] = pe.n
                    dn_done[ei] = pe.n
            p.emit(nc)
        h2T_st.close()

        with contextlib.ExitStack() as b5:
            xstg = b5.enter_context(nc.sbuf_tensor("l1f_xstg5", [128, 2, D], F32))
            nrm = b5.enter_context(nc.sbuf_tensor("l1f_nrm5", [128, 2, D], F32))
            p = Plan(g)
            omark = {}
            for t in range(NT):
                buf = t % 2
                cb1 = cols[:, buf * 16:buf * 16 + 8]
                w = ((st, omark[t - 2]),) if t - 2 in omark else ()
                p.op('sync', (lambda e, t=t, buf=buf: e.dma_start(
                    xstg[:, buf, :], xs[:, t, :])), waits=w, inc=(ld, 16))
                ldm = ld.n
                ln_stats(p, g, u2[:, t, :], sqd[:, :], cb1, g.epsc[:, 0:1])
                wv = [(ld, ldm), (dve, dve.n)]
                if t - 2 in omark:
                    wv.append((st, omark[t - 2]))
                p.op('vector', (lambda e, buf=buf, cb1=cb1: e.tensor_scalar(
                    nrm[:, buf, :], xstg[:, buf, :], cb1[:, 1:2], None,
                    OP.add)), waits=tuple(wv))
                p.op('vector', (lambda e, t=t, buf=buf, cb1=cb1:
                                e.scalar_tensor_tensor(
                    nrm[:, buf, :], u2[:, t, :], cb1[:, 0:1], nrm[:, buf, :],
                    OP.mult, OP.add)), inc=(dve, 1))
                vm = dve.n
                p.op('sync', (lambda e, t=t, buf=buf: e.dma_start(
                    xd[:, t, :], nrm[:, buf, :])), waits=((dve, vm),),
                    inc=(st, 16))
                omark[t] = st.n
            need = st.n
            p.op('sync', lambda e: e.nop(), waits=((st, need),))
            p.emit(nc)



def host_constants():
    ident = np.eye(128, dtype=np.float32)
    causal = np.where(np.tril(np.ones((128, 128), bool)), 0.0, NEG
                      ).astype(np.float32)
    inv_freq = 1.0 / ROPE_THETA ** (np.arange(0, HD, 2, dtype=np.float32) / HD)
    ang = np.arange(S, dtype=np.float32)[:, None] * inv_freq[None, :]
    cos = np.concatenate([np.cos(ang), np.cos(ang)], axis=1).T.copy()
    sin = np.concatenate([np.sin(ang), np.sin(ang)], axis=1).T.copy()
    sgn = np.ones((HD, 1), np.float32)
    sgn[:HD // 2] = -1.0
    qs = 1.0 / math.sqrt(HD)
    rep = lambda a: np.concatenate([a, a], axis=0).astype(np.float32)
    consts = {
        "ident": ident,
        "causal": causal,
        "ropeq_cos": rep(cos * qs),
        "ropeq_sin": rep(sin * sgn * qs),
        "ropek_cos": rep(cos),
        "ropek_sin": rep(sin * sgn),
        "onesr": np.ones((1, 128), np.float32),
    }
    return consts


def _pc(v, nchunk):
    return np.ascontiguousarray(np.asarray(v, np.float32).reshape(nchunk, 128).T)


def prep_inputs(inputs):
    f = lambda n: np.ascontiguousarray(np.asarray(inputs[n], np.float32))
    base = {k: f(v) for k, v in WMAP.items()}
    for v, hn in (("g1", "L0_attn_g1"), ("b1", "L0_attn_b1"),
                  ("ba", "L0_attn_ba"), ("bb", "L0_attn_bb"),
                  ("bc", "L0_attn_bc"), ("g2", "L0_attn_g2"),
                  ("b2", "L0_attn_b2"), ("bm2", "L0_attn_bm2")):
        base[v + "_pc"] = _pc(inputs[hn], 8)
    base["bm1_pc"] = _pc(inputs["L0_attn_bm1"], 32)
    rms0 = np.asarray(inputs["L0_ffn_rms_w"], np.float32)
    base["rms0_pc"] = np.ascontiguousarray(
        rms0.reshape(E0R, 8, 128).transpose(2, 0, 1).reshape(128, E0R * 8))
    rms0s = np.asarray(inputs["L0_ffn_rms_w_s"], np.float32)
    base["rms0s_pc"] = np.ascontiguousarray(
        rms0s.reshape(E0S, 8, 128).transpose(2, 0, 1).reshape(128, E0S * 8))
    rw0 = np.asarray(inputs["L0_ffn_router_w"], np.float32)
    base["rw0_pc"] = np.ascontiguousarray(
        rw0.reshape(ND, 128, E0R).transpose(1, 0, 2))
    rw1 = np.asarray(inputs["L1_ffn_router_w"], np.float32)
    base["rw1_pc"] = np.ascontiguousarray(
        rw1.reshape(ND, 128, E1).transpose(1, 0, 2))
    base["ln1_pc"] = _pc(inputs["L1_ffn_ln1_w"], 8)
    base["ln2_pc"] = _pc(inputs["L1_ffn_ln2_w"], 8)
    base.update(host_constants())
    return base


WMAP = {
    "Wa": "L0_attn_Wa", "Wb": "L0_attn_Wb", "Wc": "L0_attn_Wc",
    "bm2": "L0_attn_bm2",
    "W1": "L0_attn_W1", "W2": "L0_attn_W2",
    "f0w1": "L0_ffn_w1", "f0w2": "L0_ffn_w2",
    "f0w1s": "L0_ffn_w1_s", "f0w2s": "L0_ffn_w2_s",
    "Wq": "L1_attn_Wq", "Wk": "L1_attn_Wk", "Wv": "L1_attn_Wv",
    "wq1": "L1_ffn_wq", "wk1": "L1_ffn_wk", "wv1": "L1_ffn_wv",
    "wo1": "L1_ffn_wo",
    "wg1": "L1_ffn_w_gate", "wu1": "L1_ffn_w_up", "wd1": "L1_ffn_w_down",
}


def kernel(**inputs):
    nc, g = build_program(upto="l1f")
    x = np.ascontiguousarray(np.asarray(inputs["L0_attn_x"], np.float32))
    base = prep_inputs(inputs)
    in_maps = []
    for c in range(8):
        m = dict(base)
        m["x_in"] = x[c]
        in_maps.append(m)
    res = run_bass_kernel_spmd(nc, in_maps, core_ids=list(range(8)))
    return np.stack([r["y_out"] for r in res.results], axis=0)
